# version 16
# baseline (speedup 1.0000x reference)
import contextlib
import math
import numpy as np
import ml_dtypes
import concourse.bass as bass
import concourse.mybir as mybir
from concourse.bass_utils import run_bass_kernel_spmd
from concourse.alu_op_type import AluOpType

F32 = mybir.dt.float32
BF16 = mybir.dt.bfloat16
AF = mybir.ActivationFunctionType
OP = AluOpType

ENGINES = ("sync", "scalar", "vector", "tensor", "gpsimd")


class Prog:
    def __init__(self, nc):
        self.nc = nc
        self.es = contextlib.ExitStack()
        self.sems = {}
        self.cnt = {}
        self.waited = {e: {} for e in ENGINES}
        self.ops = {e: [] for e in ENGINES}
        self.buf = {}
        self.dma_chans = set()

    def sem(self, name):
        if name not in self.sems:
            self.sems[name] = self.es.enter_context(self.nc.semaphore(name))
            self.cnt[name] = 0
        return self.sems[name]

    def op(self, eng, fn, reads=(), writes=(), sem=None, inc=1, dma=False):
        s = sem or eng
        self.sem(s)
        if dma:
            self.dma_chans.add(s)
        deps = {}

        def need(ds, dc):
            if ds in self.dma_chans:
                dc = self.cnt[ds]
            if deps.get(ds, 0) < dc:
                deps[ds] = dc

        for b in reads:
            st = self.buf.get(b)
            if st and st["w"]:
                need(*st["w"])
        for b in writes:
            st = self.buf.get(b)
            if st:
                if st["w"]:
                    need(*st["w"])
                for rs, rc in st["r"].items():
                    need(rs, rc)
        waits = []
        for ds, dc in sorted(deps.items()):
            if eng == "tensor" and ds == "tensor":
                continue
            if self.waited[eng].get(ds, 0) >= dc:
                continue
            waits.append((ds, dc))
            self.waited[eng][ds] = dc
        self.cnt[s] += inc
        c_rec = self.cnt[s] + (1 if inc == 0 else 0)
        for b in reads:
            st = self.buf.setdefault(b, {"w": None, "r": {}})
            st["r"][s] = max(st["r"].get(s, 0), c_rec)
        for b in writes:
            self.buf[b] = {"w": (s, c_rec), "r": {}}
        self.ops[eng].append((waits, fn, s, inc))
        return c_rec

    def tick(self, eng):
        s = eng
        self.sem(s)
        self.cnt[s] += 1
        sem = self.sems[s]
        self.ops[eng].append(
            ([], lambda e, sem=sem: e.sem_inc(sem, 1), None, 0))
        return self.cnt[s]

    def wait(self, eng, sem_name, count):
        self.sem(sem_name)
        self.ops[eng].append(([(sem_name, count)], None, None, 0))
        self.waited[eng][sem_name] = max(self.waited[eng].get(sem_name, 0),
                                         count)

    def build(self):
        for eng in ENGINES:
            for waits, _, _, _ in self.ops[eng]:
                for ws, wc in waits:
                    assert wc <= self.cnt[ws], \
                        f"wait {ws}>={wc} never reached (final {self.cnt[ws]})"
        sems = self.sems
        with self.nc.Block() as block:
            for eng in ENGINES:
                if not self.ops[eng]:
                    continue
                oplist = self.ops[eng]

                def body(engobj, oplist=oplist):
                    for waits, fn, s, inc in oplist:
                        for ws, wc in waits:
                            engobj.wait_ge(sems[ws], wc)
                        if fn is not None:
                            instr = fn(engobj)
                            if inc:
                                instr.then_inc(sems[s], inc)

                getattr(block, eng)(body)



def build_program(C=2048, H=8192, T=2048, SEQS=2, TT=512):
    CC = C // 128
    HC = H // 128
    HQ = 4
    HCQ = HC // HQ
    GRP = 2
    TPS = T // TT
    NTILES = SEQS * TPS
    N = SEQS * T
    NQ = 4
    EPS = 1e-5
    SCL = 1.0 / math.sqrt(H)

    n_proj_units = 5 * NQ * (CC // 2)
    n_ffn_units = HQ * (HCQ // GRP) * (CC // 2)
    n_units = n_proj_units + n_ffn_units

    nc = bass.Bass("TRN2", target_bir_lowering=False)
    P = Prog(nc)

    x_fm = nc.dram_tensor("x_fm", [C, N], F32, kind="ExternalInput").ap()
    wcat = nc.dram_tensor("wcat", [n_units * 128, 1024], F32,
                          kind="ExternalInput").ap()
    w2t = nc.dram_tensor("w2t", [64 * (C // 1024) * 128, 1024], F32,
                         kind="ExternalInput").ap()
    vecs_in = nc.dram_tensor("vecs", [128, CC * 8], F32,
                             kind="ExternalInput").ap()
    h64_in = nc.dram_tensor("h64", [128, 128], F32, kind="ExternalInput").ap()
    h128_in = nc.dram_tensor("h128", [128, 128], F32,
                             kind="ExternalInput").ap()
    y_fm = nc.dram_tensor("y_fm", [C, N], F32, kind="ExternalOutput").ap()
    w_bf = nc.dram_tensor("w_bf", [n_units * 128, 1024], BF16,
                          kind="Internal").ap()
    w2s = nc.dram_tensor("w2s", [H, C], BF16, kind="Internal").ap()

    es = P.es
    sb = {}

    def S(name, shape, dt=F32):
        sb[name] = es.enter_context(nc.sbuf_tensor(name, shape, dt))
        return sb[name]

    xbuf = S("xbuf", [128, CC * TT], F32)
    P1 = S("P1", [128, CC * TT], F32)
    P2 = S("P2", [128, CC * TT], F32)
    B1 = S("B1", [128, CC * TT], BF16)
    B2 = S("B2", [128, CC * TT], BF16)
    attnG = S("attnG", [128, CC * TT], BF16)
    TP = [S(f"T{i}", [128, TT + 4], F32) for i in range(3)]
    mubc = S("mubc", [128, TT], F32)
    rsdbc = S("rsdbc", [128, TT], F32)
    abuf = S("abuf", [128, TT + 1], F32)
    bbuf = S("bbuf", [128, TT + 1], F32)
    NWR = 4
    wr = [S(f"wr{i}", [128, 2048], BF16) for i in range(NWR)]
    rpf = [S(f"rpf{i}", [128, 1024], F32) for i in range(3)]
    wtf = [S(f"wtf{i}", [128, 1024], F32) for i in range(2)]
    w2o2 = S("w2o2", [128, 2048], BF16)
    vecs = S("vecsb", [128, CC * 8], F32)
    aux = S("aux", [128, CC * 4], F32)
    xnb = [S(f"xnb{i}", [128, CC], F32) for i in range(2)]
    acar = S("acar", [128, CC], F32)
    bcar = S("bcar", [128, CC], F32)
    h128b = S("h128b", [128, 128], BF16)
    onesb = S("onesb", [128, 1], BF16)
    ones1 = S("ones1", [1, 128], F32)
    h64f = S("h64f", [128, 128], F32)
    h128f = S("h128f", [128, 128], F32)
    epsr = S("epsr", [1, 1], F32)
    cbf = [S(f"cbf{i}", [128, TT], BF16) for i in range(4)]
    rows = P2[0:1, 0:3 * TT]
    pb = [es.enter_context(nc.psum_tensor(f"pb{i}", [128, 512], F32))
          for i in range(8)]

    aux2 = S("aux2", [128, CC * 2], F32)
    aux3 = S("aux3", [128, CC], F32)

    def vcol(blk, cc):
        return vecs[:, blk * CC + cc:blk * CC + cc + 1]

    def acol(blk, cc):
        return aux[:, blk * CC + cc:blk * CC + cc + 1]

    def a2col(blk, cc):
        return aux2[:, blk * CC + cc:blk * CC + cc + 1]

    TM = {"k": 2, "v": 3, "r": 4, "g": 5}
    OMT = {"k": ("aux", 1), "v": ("aux", 2), "r": ("aux2", 0),
           "g": ("aux2", 1)}

    def omtcol(p, cc):
        t, blk = OMT[p]
        return acol(blk, cc) if t == "aux" else a2col(blk, cc)

    def xc(buf, cc):
        return buf[:, cc * TT:(cc + 1) * TT]

    P.op("sync", lambda e: e.dma_start(out=vecs[:], in_=vecs_in),
         writes=["vecs"], sem="cld", inc=16, dma=True)
    P.op("sync", lambda e: e.dma_start(out=h64f[:], in_=h64_in),
         writes=["h64f"], sem="cld", inc=16, dma=True)
    P.op("sync", lambda e: e.dma_start(out=h128f[:], in_=h128_in),
         writes=["h128f"], sem="cld", inc=16, dma=True)
    P.op("vector", lambda e: e.memset(onesb[:], 1.0), writes=["onesb"])
    P.op("vector", lambda e: e.memset(ones1[:], 1.0), writes=["ones1"])
    P.op("vector", lambda e: e.memset(epsr[:], EPS), writes=["epsr"])
    P.op("vector", lambda e: e.memset(acar[:], 0.0), writes=["acar"])
    P.op("vector", lambda e: e.memset(bcar[:], 0.0), writes=["bcar"])
    P.op("scalar", lambda e: e.copy(out=h128b[:], in_=h128f[:]),
         reads=["h128f"], writes=["h128b"])
    P.op("scalar", lambda e: e.activation(
            out=TP[0][:, 0:CC], in_=vecs[:, 0:CC], func=AF.Exp),
         reads=["vecs"], writes=["T0"])
    P.op("scalar", lambda e: e.activation(
            out=aux[:, 0:CC], in_=TP[0][:, 0:CC], func=AF.Exp, scale=-1.0),
         reads=["T0"], writes=["aux"])
    P.op("scalar", lambda e: e.activation(
            out=aux3[:, 0:CC], in_=vecs[:, CC:2 * CC], func=AF.Exp),
         reads=["vecs"], writes=["aux3"])
    for p in ["k", "v", "r", "g"]:
        t, blk = OMT[p]
        dst = aux if t == "aux" else aux2
        off = blk * CC
        P.op("vector", lambda e, dst=dst, off=off, p=p: e.tensor_scalar(
                out=dst[:, off:off + CC], in0=vecs[:, TM[p] * CC:(TM[p] + 1) * CC],
                scalar1=-1.0, scalar2=1.0, op0=OP.mult, op1=OP.add),
             reads=["vecs"], writes=[t])

    wcat_r = wcat.rearrange("(u p) f -> u p f", p=128)
    w_bf_r = w_bf.rearrange("(u p) f -> u p f", p=128)
    rp_state = {"seq": 0}

    w2t_r = w2t.rearrange("(hc p) co -> hc p co", p=128)
    w2s_r = w2s.rearrange("(hc p) co -> hc p co", p=128)
    n_ch = C // 1024

    tf_state = {"i": 0}
    fring = {"i": 0}

    def stage_f32(srcap):
        fsl = fring["i"] % 3
        feng = "sync" if fring["i"] % 2 == 0 else "gpsimd"
        fring["i"] += 1
        P.op(feng, lambda e, srcap=srcap, fsl=fsl: e.dma_start(
                out=rpf[fsl][:], in_=srcap),
             writes=[f"rpf{fsl}"], sem=f"rpl{fsl}", inc=16, dma=True)
        return fsl

    def emit_transform_pieces(n):
        w2tp = w2t.rearrange("(i p) f -> i p f", p=128)
        for _ in range(n):
            i = tf_state["i"]
            if i >= 64 * n_ch:
                return
            tf_state["i"] += 1
            hpp, ch = divmod(i, n_ch)
            hp0, hp1 = 2 * hpp, 2 * hpp + 1
            sl = i % 2
            P.op("gpsimd", lambda e, i=i, sl=sl, w2tp=w2tp: e.dma_start(
                    out=wtf[sl][:], in_=w2tp[i]),
                 writes=[f"wtf{sl}"], sem=f"wtl{sl}", inc=16, dma=True)
            for s2 in range(2):
                bk = 4 + s2
                o0 = sl * 1024 + s2 * 512
                P.op("tensor", lambda e, sl=sl, s2=s2, bk=bk: e.matmul(
                        pb[bk][:, :], h64f[:, :],
                        wtf[sl][:, s2 * 512:(s2 + 1) * 512],
                        start=True, stop=True),
                     reads=[f"wtf{sl}", "h64f"], writes=[f"pb{bk}"])
                P.op("vector", lambda e, o0=o0, bk=bk, sl=sl: e.tensor_scalar(
                        out=w2o2[:, o0:o0 + 512],
                        in0=pb[bk][:, :], scalar1=SCL, scalar2=None,
                        op0=OP.mult),
                     reads=[f"pb{bk}"], writes=[f"w2o2{sl}"])
            P.op("sync", lambda e, hp0=hp0, ch=ch, sl=sl: e.dma_start(
                    out=w2s_r[:, hp0, ch * 1024:(ch + 1) * 1024],
                    in_=w2o2[0:HC, sl * 1024:(sl + 1) * 1024]),
                 reads=[f"w2o2{sl}"], writes=[f"w2s_{ch}"], sem="w2st",
                 inc=16, dma=True)
            P.op("sync", lambda e, hp1=hp1, ch=ch, sl=sl: e.dma_start(
                    out=w2s_r[:, hp1, ch * 1024:(ch + 1) * 1024],
                    in_=w2o2[HC:2 * HC, sl * 1024:(sl + 1) * 1024]),
                 reads=[f"w2o2{sl}"], writes=[f"w2s_{ch}"], sem="w2st",
                 inc=16, dma=True)

    ring = {"seq": 0}
    first_use = set()
    rp_seq = {"i": 0}

    def fill_slot(units):
        sl = ring["seq"] % NWR
        ring["seq"] += 1
        name = f"wr{sl}"
        ch = f"cw{sl}"
        u0 = units[0]
        if (isinstance(u0, int) and u0 not in first_use
                and all(isinstance(u, int) for u in units)):
            assert len(units) == 2 and units[1] == u0 + 1
            first_use.update(units)
            for j, u in enumerate(units):
                fsl = stage_f32(wcat_r[u])
                P.op("vector", lambda e, sl=sl, fsl=fsl, j=j: e.tensor_copy(
                        out=wr[sl][:, j * 1024:(j + 1) * 1024],
                        in_=rpf[fsl][:]),
                     reads=[f"rpf{fsl}"], writes=[name])
            wdst = w_bf.rearrange("(u p) f -> p u f", p=128)[:, u0:u0 + 2, :]
            P.op("scalar", lambda e, sl=sl, wdst=wdst: e.dma_start(
                    out=wdst,
                    in_=wr[sl][:].rearrange("p (u f) -> p u f", f=1024)),
                 reads=[name], writes=[f"wbf_u{u0}", f"wbf_u{u0+1}"],
                 sem="rps", inc=16, dma=True)
            return sl, wr[sl]
        if isinstance(u0, int):
            for j, u in enumerate(units):
                ieng = "sync" if j == 0 else "gpsimd"
                P.op(ieng, lambda e, u=u, j=j, sl=sl: e.dma_start(
                        out=wr[sl][:, j * 1024:(j + 1) * 1024],
                        in_=w_bf_r[u]),
                     reads=[f"wbf_u{u}"], writes=[name], sem=ch, inc=16,
                     dma=True)
            return sl, wr[sl]
        for t, (r0, c0) in enumerate(units):
            ieng = "sync" if t % 2 == 0 else "gpsimd"
            P.op(ieng, lambda e, r0=r0, c0=c0, t=t, sl=sl: e.dma_start(
                    out=wr[sl][:, t * 512:(t + 1) * 512],
                    in_=w2s[r0:r0 + 128, c0:c0 + 512]),
                 reads=[f"w2s_{c0 // 1024}"],
                 writes=[name], sem=ch, inc=16, dma=True)
        return sl, wr[sl]

    PIDX = {"k": 0, "v": 1, "r": 2, "g": 3, "o": 4}

    def proj_unit(p, hq, m):
        return (PIDX[p] * NQ + hq) * (CC // 2) + m

    def ffn_unit(q, ug, m):
        return n_proj_units + (q * (HCQ // GRP) + ug) * (CC // 2) + m

    def load_x(tile, dst, dnames):
        sq, ti = divmod(tile, TPS)
        col0 = sq * T + ti * TT
        srcap = x_fm.rearrange("(cc p) n -> p cc n", p=128)[:, :, col0:col0 + TT]
        xdst = dst[:].rearrange("p (cc n) -> p cc n", n=TT)
        P.op("scalar", lambda e, srcap=srcap, xdst=xdst: e.dma_start(
                out=xdst, in_=srcap),
             reads=[], writes=dnames, sem="xld", inc=16, dma=True)

    load_x(0, xbuf, [f"xbuf_{c}" for c in range(CC)])
    for tile in range(NTILES):
        sq, ti = divmod(tile, TPS)
        t0 = ti * TT
        col0 = sq * T + t0
        bpar = tile % 2
        xb, xbn = (xbuf, "xbuf") if tile % 2 == 0 else (P1, "P1")
        pp, ppn = (P1, "P1") if tile % 2 == 0 else (xbuf, "xbuf")
        Xn = [f"{xbn}_{c}" for c in range(CC)]

        for cc in range(CC):
            cb = (cc % 2) * 2
            if cc % 2 == 0:
                P.op("scalar", lambda e, xb=xb, cc=cc, cb=cb: e.copy(
                        out=cbf[cb][:], in_=xc(xb, cc)),
                     reads=[f"{xbn}_{cc}"], writes=[f"cbf{cb}"])
                P.op("vector", lambda e, xb=xb, cc=cc, cb=cb: e.tensor_tensor(
                        out=cbf[cb + 1][:], in0=xc(xb, cc), in1=xc(xb, cc),
                        op=OP.mult),
                     reads=[f"{xbn}_{cc}"], writes=[f"cbf{cb+1}"])
            else:
                P.op("vector", lambda e, xb=xb, cc=cc, cb=cb: e.tensor_copy(
                        out=cbf[cb][:], in_=xc(xb, cc)),
                     reads=[f"{xbn}_{cc}"], writes=[f"cbf{cb}"])
                P.op("scalar", lambda e, xb=xb, cc=cc, cb=cb: e.activation(
                        out=cbf[cb + 1][:], in_=xc(xb, cc), func=AF.Square),
                     reads=[f"{xbn}_{cc}"], writes=[f"cbf{cb+1}"])
            P.op("tensor", lambda e, cc=cc, cb=cb: e.matmul(
                    pb[6][0:1, 0:TT], onesb[:], cbf[cb][:],
                    start=(cc == 0), stop=(cc == CC - 1)),
                 reads=[f"cbf{cb}", "onesb"], writes=["pb6"],
                 inc=1 if cc == CC - 1 else 0)
            P.op("tensor", lambda e, cc=cc, cb=cb: e.matmul(
                    pb[7][0:1, 0:TT], onesb[:], cbf[cb + 1][:],
                    start=(cc == 0), stop=(cc == CC - 1)),
                 reads=[f"cbf{cb+1}", "onesb"], writes=["pb7"],
                 inc=1)
        P.op("vector", lambda e, xb=xb, pp=pp: e.tensor_scalar(
                out=rows[:, 0:TT], in0=pb[6][0:1, 0:TT], scalar1=1.0 / C,
                scalar2=None, op0=OP.mult),
             reads=["pb6"], writes=["P2_0"])
        P.op("vector", lambda e, xb=xb, pp=pp: e.tensor_tensor(
                out=rows[:, 2 * TT:3 * TT], in0=rows[:, 0:TT],
                in1=rows[:, 0:TT], op=OP.mult),
             reads=["P2_0"], writes=["P2_2"])
        P.op("vector", lambda e, xb=xb, pp=pp: e.scalar_tensor_tensor(
                out=rows[:, 2 * TT:3 * TT], in0=pb[7][0:1, 0:TT],
                scalar=1.0 / C, in1=rows[:, 2 * TT:3 * TT],
                op0=OP.mult, op1=OP.subtract),
             reads=["pb7", "P2_2"], writes=["P2_2"])
        P.op("scalar", lambda e, xb=xb, pp=pp: e.activation(
                out=rows[:, 2 * TT:3 * TT], in_=rows[:, 2 * TT:3 * TT],
                func=AF.Ln, bias=epsr[0:1, 0:1]),
             reads=["P2_2", "epsr"], writes=["P2_2"])
        P.op("scalar", lambda e, xb=xb, pp=pp: e.activation(
                out=rows[:, TT:2 * TT], in_=rows[:, 2 * TT:3 * TT],
                func=AF.Exp, scale=-0.5),
             reads=["P2_2"], writes=["P2_1"])
        P.op("tensor", lambda e, xb=xb, pp=pp: e.matmul(pb[4][:, 0:TT], ones1[:],
                                          rows[:, 0:TT], start=True, stop=True),
             reads=["P2_0", "ones1"], writes=["pb4"])
        P.op("tensor", lambda e, xb=xb, pp=pp: e.matmul(pb[5][:, 0:TT], ones1[:],
                                          rows[:, TT:2 * TT], start=True,
                                          stop=True),
             reads=["P2_1", "ones1"], writes=["pb5"])
        P.op("vector", lambda e: e.tensor_copy(out=mubc[:], in_=pb[4][:, 0:TT]),
             reads=["pb4"], writes=["mubc"])
        P.op("vector", lambda e: e.tensor_copy(out=rsdbc[:], in_=pb[5][:, 0:TT]),
             reads=["pb5"], writes=["rsdbc"])

        for cc in range(CC):
            slot = cc % 2
            P.op("vector", lambda e, xb=xb, cc=cc, slot=slot: e.tensor_tensor(
                    out=TP[slot][:, 0:TT], in0=xc(xb, cc), in1=mubc[:],
                    op=OP.subtract),
                 reads=[f"{xbn}_{cc}", "mubc"], writes=[f"T{slot}"])
            P.op("vector", lambda e, cc=cc, slot=slot: e.tensor_tensor(
                    out=xc(attnG, cc), in0=TP[slot][:, 0:TT],
                    in1=rsdbc[:], op=OP.mult),
                 reads=[f"T{slot}", "rsdbc"], writes=[f"xnbf_{cc}"])
            if t0 == 0:
                P.op("vector", lambda e, cc=cc, bpar=bpar: e.tensor_copy(
                        out=xnb[bpar][:, cc:cc + 1], in_=xc(attnG, cc)[:, 1:2]),
                     reads=[f"xnbf_{cc}"], writes=[f"xnb{bpar}"])
            P.op("vector", lambda e, cc=cc, bpar=bpar: e.tensor_copy(
                    out=xnb[1 - bpar][:, cc:cc + 1],
                    in_=xc(attnG, cc)[:, TT - 1:TT]),
                 reads=[f"xnbf_{cc}"], writes=[f"xnb{1-bpar}"])

        def build_operand(p, B, Bn):
            for cc in range(CC):
                slot = cc % 2
                tn = f"T{slot}"
                P.op("vector", lambda e, cc=cc, slot=slot, p=p:
                     e.tensor_scalar(
                        out=TP[slot][:, 0:TT], in0=xc(attnG, cc),
                        scalar1=omtcol(p, cc), scalar2=None, op0=OP.mult),
                     reads=[f"xnbf_{cc}", OMT[p][0]], writes=[tn])
                P.op("vector", lambda e, cc=cc, slot=slot, p=p:
                     e.scalar_tensor_tensor(
                        out=xc(B, cc)[:, 1:TT], in0=xc(attnG, cc)[:, 0:TT - 1],
                        scalar=vcol(TM[p], cc), in1=TP[slot][:, 1:TT],
                        op0=OP.mult, op1=OP.add),
                     reads=[f"xnbf_{cc}", tn], writes=[f"{Bn}_{cc}"])
                P.op("vector", lambda e, cc=cc, slot=slot, p=p, bpar=bpar:
                     e.scalar_tensor_tensor(
                        out=xc(B, cc)[:, 0:1], in0=xnb[bpar][:, cc:cc + 1],
                        scalar=vcol(TM[p], cc), in1=TP[slot][:, 0:1],
                        op0=OP.mult, op1=OP.add),
                     reads=[f"xnb{bpar}", tn], writes=[f"{Bn}_{cc}"])
                if tile == 0:
                    emit_transform_pieces(2)

        def proj_matmuls(p, B, Bn, evac):
            for hq in range(NQ):
                bs = (hq % 2) * 4
                for kc in range(CC):
                    if kc % 4 == 0:
                        sl, wrt = fill_slot([proj_unit(p, hq, kc // 2),
                                             proj_unit(p, hq, kc // 2 + 1)])
                    uoff = (kc % 4) * 512
                    for ci in range(4):
                        P.op("tensor",
                             lambda e, sl=sl, uoff=uoff, ci=ci, kc=kc, bs=bs:
                             e.matmul(
                                pb[bs + ci][:, 0:TT],
                                wr[sl][:, uoff + ci * 128:uoff + (ci + 1) * 128],
                                xc(B, kc),
                                start=(kc == 0), stop=(kc == CC - 1)),
                             reads=[f"wr{sl}", f"{Bn}_{kc}"],
                             writes=[f"pb{bs+ci}"],
                             inc=1 if (kc == CC - 1 or
                                       (kc % 4 == 3 and ci == 3))
                             else 0)
                for ci in range(4):
                    evac(hq * 4 + ci, bs + ci)

        def ffn_up_group(q, ug):
            bs = (ug % 2) * 4
            for kc in range(CC):
                if kc % 4 == 0:
                    sl, wrt = fill_slot([ffn_unit(q, ug, kc // 2),
                                         ffn_unit(q, ug, kc // 2 + 1)])
                uoff = (kc % 4) * 512
                for hj in range(GRP):
                    P.op("tensor",
                         lambda e, sl=sl, uoff=uoff, hj=hj, kc=kc, bs=bs:
                         e.matmul(
                            pb[bs + 2 * hj][:, 0:TT],
                            wr[sl][:, uoff + hj * 128:uoff + (hj + 1) * 128],
                            xc(attnG, kc), start=(kc == 0),
                            stop=(kc == CC - 1)),
                         reads=[f"wr{sl}", f"xnbf_{kc}"],
                         writes=[f"pb{bs+2*hj}"],
                         inc=1 if kc == CC - 1 else 0)
                    P.op("tensor",
                         lambda e, sl=sl, uoff=uoff, hj=hj, kc=kc, bs=bs:
                         e.matmul(
                            pb[bs + 2 * hj + 1][:, 0:TT],
                            wr[sl][:, uoff + 256 + hj * 128:
                                   uoff + 256 + (hj + 1) * 128],
                            xc(attnG, kc), start=(kc == 0),
                            stop=(kc == CC - 1)),
                         reads=[f"wr{sl}", f"xnbf_{kc}"],
                         writes=[f"pb{bs+2*hj+1}"],
                         inc=1 if (kc == CC - 1 or
                                   (kc % 4 == 3 and hj == GRP - 1))
                         else 0)
            for hj in range(GRP):
                j = ug * GRP + hj
                slot = hj % 2
                P.op("scalar", lambda e, hj=hj, slot=slot, bs=bs: e.activation(
                        out=TP[slot][:, 0:TT], in_=pb[bs + 2 * hj][:, 0:TT],
                        func=AF.Silu),
                     reads=[f"pb{bs+2*hj}"], writes=[f"T{slot}"])
                P.op("vector", lambda e, hj=hj, j=j, slot=slot, bs=bs:
                     e.tensor_tensor(
                        out=xc(B2, j), in0=TP[slot][:, 0:TT],
                        in1=pb[bs + 2 * hj + 1][:, 0:TT], op=OP.mult),
                     reads=[f"T{slot}", f"pb{bs+2*hj+1}"],
                     writes=[f"B2_{j}"])

        def ffn_h128(q):
            for j in range(HCQ):
                bk = j % 4
                P.op("tensor", lambda e, j=j, bk=bk: e.matmul(
                        pb[bk][:, 0:TT], h128b[:], xc(B2, j),
                        start=True, stop=True),
                     reads=["h128b", f"B2_{j}"], writes=[f"pb{bk}"])
                P.op("scalar", lambda e, j=j, bk=bk: e.activation(
                        out=xc(B2, j), in_=pb[bk][:, 0:TT], func=AF.Copy,
                        scale=1.0),
                     reads=[f"pb{bk}"], writes=[f"B2_{j}"])

        def ffn_down(q):
            for hq in range(NQ):
                bs = (hq % 2) * 4
                ccs = range(hq * 4, (hq + 1) * 4)
                for j in range(HCQ):
                    kc2 = q * HCQ + j
                    if j % 4 == 0:
                        sl, wrt = fill_slot(
                            [((kc2 + t) * 128, hq * 512) for t in range(4)])
                    uoff = (j % 4) * 512
                    for ci, cc in enumerate(ccs):
                        P.op("tensor",
                             lambda e, sl=sl, uoff=uoff, ci=ci, j=j, bs=bs:
                             e.matmul(
                                pb[bs + ci][:, 0:TT],
                                wr[sl][:, uoff + ci * 128:uoff + (ci + 1) * 128],
                                xc(B2, j), start=(j == 0),
                                stop=(j == HCQ - 1)),
                             reads=[f"wr{sl}", f"B2_{j}"],
                             writes=[f"pb{bs+ci}"],
                             inc=1 if (j == HCQ - 1 or
                                       (j % 4 == 3 and ci == 3))
                             else 0)
                for ci, cc in enumerate(ccs):
                    P.op("vector", lambda e, xb=xb, cc=cc, ci=ci, bs=bs: e.tensor_tensor(
                            out=xc(xb, cc), in0=xc(xb, cc),
                            in1=pb[bs + ci][:, 0:TT], op=OP.add),
                         reads=[f"{xbn}_{cc}", f"pb{bs+ci}"], writes=[f"{xbn}_{cc}"])

        build_operand("k", B1, "B1")

        def evac_k(cc, bank):
            P.op("scalar", lambda e, pp=pp, cc=cc, bank=bank: e.copy(
                    out=xc(pp, cc), in_=pb[bank][:, 0:TT]),
                 reads=[f"pb{bank}"], writes=[f"{ppn}_{cc}"])
        proj_matmuls("k", B1, "B1", evac_k)

        build_operand("v", B2, "B2")

        def evac_v(cc, bank):
            P.op("scalar", lambda e, cc=cc, bank=bank: e.copy(
                    out=xc(P2, cc), in_=pb[bank][:, 0:TT]),
                 reads=[f"pb{bank}"], writes=[f"P2_{cc}"])
        proj_matmuls("v", B2, "B2", evac_v)

        build_operand("r", B1, "B1")

        if ti == 0:
            P.op("vector", lambda e: e.memset(acar[:], 0.0), writes=["acar"])
            P.op("vector", lambda e: e.memset(bcar[:], 0.0), writes=["bcar"])
        for cc in range(CC):
            P.op("scalar", lambda e, pp=pp, cc=cc: e.activation(
                    out=TP[2][:, 0:TT], in_=xc(pp, cc), func=AF.Exp),
                 reads=[f"{ppn}_{cc}"], writes=["T2"])
            P.op("vector", lambda e, cc=cc: e.tensor_tensor(
                    out=xc(P2, cc), in0=TP[2][:, 0:TT], in1=xc(P2, cc),
                    op=OP.mult),
                 reads=["T2", f"P2_{cc}"], writes=[f"P2_{cc}"])
            P.op("vector", lambda e, cc=cc: e.tensor_copy(
                    out=abuf[:, 0:1], in_=acar[:, cc:cc + 1]),
                 reads=["acar"], writes=["abuf"])
            P.op("vector", lambda e, cc=cc: e.tensor_tensor_scan(
                    out=abuf[:, 1:TT + 1],
                    data0=acol(0, cc).broadcast_to((128, TT)),
                    data1=xc(P2, cc), initial=acar[:, cc:cc + 1],
                    op0=OP.mult, op1=OP.add),
                 reads=["aux", f"P2_{cc}", "acar"], writes=["abuf"])
            P.op("vector", lambda e, cc=cc: e.tensor_copy(
                    out=acar[:, cc:cc + 1], in_=abuf[:, TT:TT + 1]),
                 reads=["abuf"], writes=["acar"])
            P.op("vector", lambda e, cc=cc: e.tensor_copy(
                    out=bbuf[:, 0:1], in_=bcar[:, cc:cc + 1]),
                 reads=["bcar"], writes=["bbuf"])
            P.op("vector", lambda e, cc=cc: e.tensor_tensor_scan(
                    out=bbuf[:, 1:TT + 1],
                    data0=acol(0, cc).broadcast_to((128, TT)),
                    data1=TP[2][:, 0:TT], initial=bcar[:, cc:cc + 1],
                    op0=OP.mult, op1=OP.add),
                 reads=["aux", "T2", "bcar"], writes=["bbuf"])
            P.op("vector", lambda e, cc=cc: e.tensor_copy(
                    out=bcar[:, cc:cc + 1], in_=bbuf[:, TT:TT + 1]),
                 reads=["bbuf"], writes=["bcar"])
            P.op("vector", lambda e, cc=cc: e.scalar_tensor_tensor(
                    out=abuf[:, 0:TT], in0=xc(P2, cc),
                    scalar=aux3[:, cc:cc + 1], in1=abuf[:, 0:TT],
                    op0=OP.mult, op1=OP.add),
                 reads=[f"P2_{cc}", "aux3", "abuf"], writes=["abuf"])
            P.op("vector", lambda e, cc=cc: e.scalar_tensor_tensor(
                    out=bbuf[:, 0:TT], in0=TP[2][:, 0:TT],
                    scalar=aux3[:, cc:cc + 1], in1=bbuf[:, 0:TT],
                    op0=OP.mult, op1=OP.add),
                 reads=["T2", "aux3", "bbuf"], writes=["bbuf"])
            P.op("scalar", lambda e, cc=cc: e.activation(
                    out=TP[2][:, 0:TT], in_=bbuf[:, 0:TT], func=AF.Ln),
                 reads=["bbuf"], writes=["T2"])
            P.op("scalar", lambda e, cc=cc: e.activation(
                    out=TP[2][:, 0:TT], in_=TP[2][:, 0:TT], func=AF.Exp,
                    scale=-1.0),
                 reads=["T2"], writes=["T2"])
            P.op("vector", lambda e, pp=pp, cc=cc: e.tensor_tensor(
                    out=xc(pp, cc), in0=abuf[:, 0:TT], in1=TP[2][:, 0:TT],
                    op=OP.mult),
                 reads=["abuf", "T2"], writes=[f"{ppn}_{cc}"])
            if cc % GRP == GRP - 1 and tile > 0:
                ffn_up_group(0, cc // GRP)
            if tile == 0:
                emit_transform_pieces(2)

        if tile > 0:
            ffn_h128(0)
            ffn_down(0)

        def evac_r(cc, bank):
            slot = cc % 2
            P.op("scalar", lambda e, cc=cc, bank=bank, slot=slot:
                 e.activation(out=TP[slot][:, 0:TT], in_=pb[bank][:, 0:TT],
                              func=AF.Sigmoid),
                 reads=[f"pb{bank}"], writes=[f"T{slot}"])
            P.op("vector", lambda e, pp=pp, cc=cc, slot=slot: e.tensor_tensor(
                    out=xc(pp, cc), in0=xc(pp, cc), in1=TP[slot][:, 0:TT],
                    op=OP.mult),
                 reads=[f"{ppn}_{cc}", f"T{slot}"], writes=[f"{ppn}_{cc}"])
        proj_matmuls("r", B1, "B1", evac_r)

        build_operand("g", B2, "B2")

        def evac_g(cc, bank):
            slot = cc % 2
            P.op("scalar", lambda e, cc=cc, bank=bank, slot=slot:
                 e.activation(out=TP[slot][:, 0:TT], in_=pb[bank][:, 0:TT],
                              func=AF.Sigmoid),
                 reads=[f"pb{bank}"], writes=[f"T{slot}"])
            P.op("vector", lambda e, pp=pp, cc=cc, slot=slot: e.tensor_tensor(
                    out=xc(B1, cc), in0=xc(pp, cc), in1=TP[slot][:, 0:TT],
                    op=OP.mult),
                 reads=[f"{ppn}_{cc}", f"T{slot}"], writes=[f"B1_{cc}"])
        proj_matmuls("g", B2, "B2", evac_g)

        def evac_o(cc, bank):
            P.op("scalar", lambda e, pp=pp, cc=cc, bank=bank: e.copy(
                    out=xc(pp, cc), in_=pb[bank][:, 0:TT]),
                 reads=[f"pb{bank}"], writes=[f"{ppn}_{cc}"])
        proj_matmuls("o", B1, "B1", evac_o)

        for cc in range(CC):
            cb = (cc % 2) * 2
            if cc % 2 == 0:
                P.op("scalar", lambda e, pp=pp, cc=cc, cb=cb: e.copy(
                        out=cbf[cb][:], in_=xc(pp, cc)),
                     reads=[f"{ppn}_{cc}"], writes=[f"cbf{cb}"])
                P.op("vector", lambda e, pp=pp, cc=cc, cb=cb: e.tensor_tensor(
                        out=cbf[cb + 1][:], in0=xc(pp, cc), in1=xc(pp, cc),
                        op=OP.mult),
                     reads=[f"{ppn}_{cc}"], writes=[f"cbf{cb+1}"])
            else:
                P.op("vector", lambda e, pp=pp, cc=cc, cb=cb: e.tensor_copy(
                        out=cbf[cb][:], in_=xc(pp, cc)),
                     reads=[f"{ppn}_{cc}"], writes=[f"cbf{cb}"])
                P.op("scalar", lambda e, pp=pp, cc=cc, cb=cb: e.activation(
                        out=cbf[cb + 1][:], in_=xc(pp, cc), func=AF.Square),
                     reads=[f"{ppn}_{cc}"], writes=[f"cbf{cb+1}"])
            P.op("tensor", lambda e, cc=cc, cb=cb: e.matmul(
                    pb[6][0:1, 0:TT], onesb[:], cbf[cb][:],
                    start=(cc == 0), stop=(cc == CC - 1)),
                 reads=[f"cbf{cb}", "onesb"], writes=["pb6"],
                 inc=1 if cc == CC - 1 else 0)
            P.op("tensor", lambda e, cc=cc, cb=cb: e.matmul(
                    pb[7][0:1, 0:TT], onesb[:], cbf[cb + 1][:],
                    start=(cc == 0), stop=(cc == CC - 1)),
                 reads=[f"cbf{cb+1}", "onesb"], writes=["pb7"],
                 inc=1)
        P.op("vector", lambda e: e.tensor_scalar(
                out=rows[:, 2 * TT:3 * TT], in0=pb[6][0:1, 0:TT],
                scalar1=1.0 / C, scalar2=None, op0=OP.mult),
             reads=["pb6"], writes=["P2_2"])
        P.op("vector", lambda e: e.tensor_tensor(
                out=rows[:, 0:TT], in0=rows[:, 2 * TT:3 * TT],
                in1=rows[:, 2 * TT:3 * TT], op=OP.mult),
             reads=["P2_2"], writes=["P2_0"])
        P.op("vector", lambda e: e.scalar_tensor_tensor(
                out=rows[:, 0:TT], in0=pb[7][0:1, 0:TT],
                scalar=1.0 / C, in1=rows[:, 0:TT],
                op0=OP.mult, op1=OP.subtract),
             reads=["pb7", "P2_0"], writes=["P2_0"])
        P.op("scalar", lambda e: e.activation(
                out=rows[:, 0:TT], in_=rows[:, 0:TT],
                func=AF.Ln, bias=epsr[0:1, 0:1]),
             reads=["P2_0", "epsr"], writes=["P2_0"])
        P.op("scalar", lambda e: e.activation(
                out=rows[:, 0:TT], in_=rows[:, 0:TT],
                func=AF.Exp, scale=-0.5),
             reads=["P2_0"], writes=["P2_0"])
        P.op("tensor", lambda e: e.matmul(pb[4][:, 0:TT], ones1[:],
                                          rows[:, 2 * TT:3 * TT], start=True,
                                          stop=True),
             reads=["P2_2", "ones1"], writes=["pb4"])
        P.op("tensor", lambda e: e.matmul(pb[5][:, 0:TT], ones1[:],
                                          rows[:, 0:TT], start=True,
                                          stop=True),
             reads=["P2_0", "ones1"], writes=["pb5"])
        P.op("vector", lambda e: e.tensor_copy(out=mubc[:], in_=pb[4][:, 0:TT]),
             reads=["pb4"], writes=["mubc"])
        P.op("vector", lambda e: e.tensor_copy(out=rsdbc[:], in_=pb[5][:, 0:TT]),
             reads=["pb5"], writes=["rsdbc"])
        for cc in range(CC):
            slot = cc % 2
            P.op("vector", lambda e, xb=xb, cc=cc: e.tensor_scalar(
                    out=xc(xb, cc), in0=xc(xb, cc), scalar1=vcol(7, cc),
                    scalar2=None, op0=OP.add),
                 reads=[f"{xbn}_{cc}", "vecs"], writes=[f"{xbn}_{cc}"])
            P.op("vector", lambda e, pp=pp, cc=cc, slot=slot: e.tensor_tensor(
                    out=TP[slot][:, 0:TT], in0=xc(pp, cc), in1=mubc[:],
                    op=OP.subtract),
                 reads=[f"{ppn}_{cc}", "mubc"], writes=[f"T{slot}"])
            P.op("vector", lambda e, cc=cc, slot=slot: e.tensor_tensor(
                    out=TP[slot][:, 0:TT], in0=TP[slot][:, 0:TT],
                    in1=rsdbc[:], op=OP.mult),
                 reads=[f"T{slot}", "rsdbc"], writes=[f"T{slot}"])
            P.op("vector", lambda e, xb=xb, cc=cc, slot=slot: e.scalar_tensor_tensor(
                    out=xc(xb, cc), in0=TP[slot][:, 0:TT],
                    scalar=vcol(6, cc), in1=xc(xb, cc),
                    op0=OP.mult, op1=OP.add),
                 reads=[f"T{slot}", "vecs", f"{xbn}_{cc}"], writes=[f"{xbn}_{cc}"])

        if tile + 1 < NTILES:
            load_x(tile + 1, pp, [f"{ppn}_{c}" for c in range(CC)])

        for q in range(0 if tile == 0 else 1, HQ):
            for ug in range(HCQ // GRP):
                ffn_up_group(q, ug)
            ffn_h128(q)
            ffn_down(q)

        dst = y_fm.rearrange("(cc p) n -> p cc n", p=128)[:, :, col0:col0 + TT]
        xsrc = xb[:].rearrange("p (cc n) -> p cc n", n=TT)
        P.op("scalar", lambda e, xb=xb, pp=pp, dst=dst, xsrc=xsrc: e.dma_start(out=dst, in_=xsrc),
             reads=Xn, writes=[], sem="xst", inc=16, dma=True)

    P.wait("sync", "xst", 16 * NTILES)
    P.build()
    meta = dict(C=C, H=H, T=T, SEQS=SEQS, TT=TT, n_units=n_units,
                n_proj_units=n_proj_units, NQ=NQ, CC=CC, HC=HC,
                HQ=HQ, HCQ=HCQ, GRP=GRP)
    return nc, meta



def pack_weights(Wr, Wk, Wv, Wg, Wo, W1, W3, meta):
    CC, NQ, HQ, HCQ, GRP = (meta["CC"], meta["NQ"], meta["HQ"],
                            meta["HCQ"], meta["GRP"])
    units = []
    for W in [Wk, Wv, Wr, Wg, Wo]:
        WT = np.ascontiguousarray(W.T)
        for hq in range(NQ):
            cs = slice(hq * 512, (hq + 1) * 512)
            for m in range(CC // 2):
                u = np.concatenate(
                    [WT[(2 * m) * 128:(2 * m + 1) * 128, cs],
                     WT[(2 * m + 1) * 128:(2 * m + 2) * 128, cs]], axis=1)
                units.append(u)
    W1T = np.ascontiguousarray(W1.T)
    W3T = np.ascontiguousarray(W3.T)
    for q in range(HQ):
        for ug in range(HCQ // GRP):
            base = (q * HCQ + ug * GRP) * 128
            for m in range(CC // 2):
                blocks = []
                for kc in (2 * m, 2 * m + 1):
                    blocks.append(W1T[kc * 128:(kc + 1) * 128,
                                      base:base + 256])
                    blocks.append(W3T[kc * 128:(kc + 1) * 128,
                                      base:base + 256])
                units.append(np.concatenate(blocks, axis=1))
    assert len(units) == meta["n_units"]
    return np.ascontiguousarray(np.concatenate(units, axis=0))


def pack_w2t(W2, meta):
    HC = meta["HC"]
    C = W2.shape[0]
    n_ch = C // 1024
    W2T = np.ascontiguousarray(W2.T)
    v = W2T.reshape(HC, 64, 2, n_ch, 1024)
    v = v.transpose(1, 3, 2, 0, 4)
    return np.ascontiguousarray(v.reshape(64 * n_ch * 128, 1024))


def hadamard_blockdiag128(hc):
    h = hadamard(hc)
    d = np.zeros((128, 128), np.float32)
    d[:hc, :hc] = h
    if 2 * hc <= 128:
        d[hc:2 * hc, hc:2 * hc] = h
    return d


def hadamard(n):
    h = np.array([[1.0]], dtype=np.float32)
    while h.shape[0] < n:
        h = np.block([[h, h], [h, -h]])
    return h.astype(np.float32)


def pack_vecs(td, tf, tmk, tmv, tmr, tmg, lng, lnb, CC):
    cols = []
    for v in [td, tf, tmk, tmv, tmr, tmg, lng, lnb]:
        cols.append(np.asarray(v, np.float32).reshape(CC, 128).T)
    return np.ascontiguousarray(np.concatenate(cols, axis=1))



_CACHE = {}


def kernel(L0_attn_x, L0_attn_time_decay, L0_attn_time_first, L0_attn_tm_k,
           L0_attn_tm_v, L0_attn_tm_r, L0_attn_tm_g, L0_attn_Wr, L0_attn_Wk,
           L0_attn_Wv, L0_attn_Wg, L0_attn_Wo, L0_attn_ln_g, L0_attn_ln_b,
           L0_ffn_W1, L0_ffn_W2, L0_ffn_W3):
    x = np.asarray(L0_attn_x, np.float32)
    B, T, C = x.shape
    H = L0_ffn_W1.shape[0]
    n_cores = 8
    SEQS = B // n_cores

    if "prog" not in _CACHE:
        _CACHE["prog"] = build_program(C=C, H=H, T=T, SEQS=SEQS, TT=512)
    nc, meta = _CACHE["prog"]

    wcat = pack_weights(np.asarray(L0_attn_Wr, np.float32),
                        np.asarray(L0_attn_Wk, np.float32),
                        np.asarray(L0_attn_Wv, np.float32),
                        np.asarray(L0_attn_Wg, np.float32),
                        np.asarray(L0_attn_Wo, np.float32),
                        np.asarray(L0_ffn_W1, np.float32),
                        np.asarray(L0_ffn_W3, np.float32), meta)
    w2t = pack_w2t(np.asarray(L0_ffn_W2, np.float32), meta)
    vecs = pack_vecs(L0_attn_time_decay, L0_attn_time_first, L0_attn_tm_k,
                     L0_attn_tm_v, L0_attn_tm_r, L0_attn_tm_g, L0_attn_ln_g,
                     L0_attn_ln_b, meta["CC"])
    h64 = hadamard(meta["HC"])
    h64d = np.zeros((128, 128), np.float32)
    h64d[:64, :64] = h64
    h64d[64:, 64:] = h64
    h128 = hadamard(128)

    in_maps = []
    for c in range(n_cores):
        xs = x[c * SEQS:(c + 1) * SEQS]
        x_fm = np.ascontiguousarray(
            xs.transpose(2, 0, 1).reshape(C, SEQS * T))
        in_maps.append({"x_fm": x_fm, "wcat": wcat, "w2t": w2t,
                        "vecs": vecs, "h64": h64d, "h128": h128})

    res = run_bass_kernel_spmd(nc, in_maps, core_ids=list(range(n_cores)))
    out = np.empty((B, T, C), np.float32)
    for c in range(n_cores):
        y = np.asarray(res.results[c]["y_fm"], np.float32)
        out[c * SEQS:(c + 1) * SEQS] = (
            y.reshape(C, SEQS, T).transpose(1, 2, 0))
    return out


def _run(nc, ins, n_cores=1, sim=False, trace=False):
    if sim:
        from concourse.bass_interp import CoreSim
        from concourse.libnrt import get_pod_config
        nc.finalize()
        sim_ = CoreSim(nc, require_finite=False, require_nnan=False,
                       **get_pod_config())
        for k, v in ins[0].items():
            sim_.tensor(k)[:] = v
        sim_.simulate(check_with_hw=False)
        return [dict(_sim=sim_)]
    res = run_bass_kernel_spmd(nc, ins, core_ids=list(range(n_cores)),
                               trace=trace)
    if trace:
        return res
    return res.results


# version 17
# speedup vs baseline: 1.0270x; 1.0270x over previous
import contextlib
import math
import numpy as np
import ml_dtypes
import concourse.bass as bass
import concourse.mybir as mybir
from concourse.bass_utils import run_bass_kernel_spmd
from concourse.alu_op_type import AluOpType

F32 = mybir.dt.float32
BF16 = mybir.dt.bfloat16
AF = mybir.ActivationFunctionType
OP = AluOpType

ENGINES = ("sync", "scalar", "vector", "tensor", "gpsimd")


class Prog:
    def __init__(self, nc):
        self.nc = nc
        self.es = contextlib.ExitStack()
        self.sems = {}
        self.cnt = {}
        self.waited = {e: {} for e in ENGINES}
        self.ops = {e: [] for e in ENGINES}
        self.buf = {}
        self.dma_chans = set()

    def sem(self, name):
        if name not in self.sems:
            self.sems[name] = self.es.enter_context(self.nc.semaphore(name))
            self.cnt[name] = 0
        return self.sems[name]

    def op(self, eng, fn, reads=(), writes=(), sem=None, inc=1, dma=False):
        s = sem or eng
        self.sem(s)
        if dma:
            self.dma_chans.add(s)
        deps = {}

        def need(ds, dc):
            if ds in self.dma_chans:
                dc = self.cnt[ds]
            if deps.get(ds, 0) < dc:
                deps[ds] = dc

        for b in reads:
            st = self.buf.get(b)
            if st and st["w"]:
                need(*st["w"])
        for b in writes:
            st = self.buf.get(b)
            if st:
                if st["w"]:
                    need(*st["w"])
                for rs, rc in st["r"].items():
                    need(rs, rc)
        waits = []
        for ds, dc in sorted(deps.items()):
            if eng == "tensor" and ds == "tensor":
                continue
            if self.waited[eng].get(ds, 0) >= dc:
                continue
            waits.append((ds, dc))
            self.waited[eng][ds] = dc
        self.cnt[s] += inc
        c_rec = self.cnt[s] + (1 if inc == 0 else 0)
        for b in reads:
            st = self.buf.setdefault(b, {"w": None, "r": {}})
            st["r"][s] = max(st["r"].get(s, 0), c_rec)
        for b in writes:
            self.buf[b] = {"w": (s, c_rec), "r": {}}
        self.ops[eng].append((waits, fn, s, inc))
        return c_rec

    def tick(self, eng):
        s = eng
        self.sem(s)
        self.cnt[s] += 1
        sem = self.sems[s]
        self.ops[eng].append(
            ([], lambda e, sem=sem: e.sem_inc(sem, 1), None, 0))
        return self.cnt[s]

    def wait(self, eng, sem_name, count):
        self.sem(sem_name)
        self.ops[eng].append(([(sem_name, count)], None, None, 0))
        self.waited[eng][sem_name] = max(self.waited[eng].get(sem_name, 0),
                                         count)

    def build(self):
        for eng in ENGINES:
            for waits, _, _, _ in self.ops[eng]:
                for ws, wc in waits:
                    assert wc <= self.cnt[ws], \
                        f"wait {ws}>={wc} never reached (final {self.cnt[ws]})"
        sems = self.sems
        with self.nc.Block() as block:
            for eng in ENGINES:
                if not self.ops[eng]:
                    continue
                oplist = self.ops[eng]

                def body(engobj, oplist=oplist):
                    for waits, fn, s, inc in oplist:
                        for ws, wc in waits:
                            engobj.wait_ge(sems[ws], wc)
                        if fn is not None:
                            instr = fn(engobj)
                            if inc:
                                instr.then_inc(sems[s], inc)

                getattr(block, eng)(body)



def build_program(C=2048, H=8192, T=2048, SEQS=2, TT=512):
    CC = C // 128
    HC = H // 128
    HQ = 4
    HCQ = HC // HQ
    GRP = 4
    TPS = T // TT
    NTILES = SEQS * TPS
    N = SEQS * T
    NHALF = (CC + 7) // 8
    EPS = 1e-5
    SCL = 1.0 / math.sqrt(H)

    n_proj_units = 5 * NHALF * CC
    n_ffn_units = HQ * (HCQ // GRP) * CC
    n_units = n_proj_units + n_ffn_units

    nc = bass.Bass("TRN2", target_bir_lowering=False)
    P = Prog(nc)

    x_fm = nc.dram_tensor("x_fm", [C, N], F32, kind="ExternalInput").ap()
    wcat = nc.dram_tensor("wcat", [n_units * 128, 1024], F32,
                          kind="ExternalInput").ap()
    w2t = nc.dram_tensor("w2t", [64 * (C // 1024) * 128, 1024], F32,
                         kind="ExternalInput").ap()
    vecs_in = nc.dram_tensor("vecs", [128, CC * 8], F32,
                             kind="ExternalInput").ap()
    h64_in = nc.dram_tensor("h64", [128, 128], F32, kind="ExternalInput").ap()
    h128_in = nc.dram_tensor("h128", [128, 128], F32,
                             kind="ExternalInput").ap()
    y_fm = nc.dram_tensor("y_fm", [C, N], F32, kind="ExternalOutput").ap()
    w_bf = nc.dram_tensor("w_bf", [n_units * 128, 1024], BF16,
                          kind="Internal").ap()
    w2s = nc.dram_tensor("w2s", [H, C], BF16, kind="Internal").ap()

    es = P.es
    sb = {}

    def S(name, shape, dt=F32):
        sb[name] = es.enter_context(nc.sbuf_tensor(name, shape, dt))
        return sb[name]

    xbuf = S("xbuf", [128, CC * TT], F32)
    P1 = S("P1", [128, CC * TT], F32)
    P2 = S("P2", [128, CC * TT], F32)
    B1 = S("B1", [128, CC * TT], BF16)
    B2 = S("B2", [128, CC * TT], BF16)
    attnG = S("attnG", [128, CC * TT], BF16)
    TP = [S(f"T{i}", [128, TT + 4], F32) for i in range(3)]
    mubc = S("mubc", [128, TT], F32)
    rsdbc = S("rsdbc", [128, TT], F32)
    abuf = S("abuf", [128, TT + 1], F32)
    bbuf = S("bbuf", [128, TT + 1], F32)
    NWR = 4
    wr = [S(f"wr{i}", [128, 2048], BF16) for i in range(NWR)]
    rpf = [S(f"rpf{i}", [128, 1024], F32) for i in range(3)]
    wtf = [S(f"wtf{i}", [128, 1024], F32) for i in range(2)]
    w2o2 = S("w2o2", [128, 2048], BF16)
    vecs = S("vecsb", [128, CC * 8], F32)
    aux = S("aux", [128, CC * 4], F32)
    xnb = [S(f"xnb{i}", [128, CC], F32) for i in range(2)]
    acar = S("acar", [128, CC], F32)
    bcar = S("bcar", [128, CC], F32)
    h128b = S("h128b", [128, 128], BF16)
    onesb = S("onesb", [128, 1], BF16)
    ones1 = S("ones1", [1, 128], F32)
    h64f = S("h64f", [128, 128], F32)
    h128f = S("h128f", [128, 128], F32)
    epsr = S("epsr", [1, 1], F32)
    cbf = [S(f"cbf{i}", [128, TT], BF16) for i in range(4)]
    rows = P2[0:1, 0:3 * TT]
    pb = [es.enter_context(nc.psum_tensor(f"pb{i}", [128, 512], F32))
          for i in range(8)]

    aux2 = S("aux2", [128, CC * 2], F32)
    aux3 = S("aux3", [128, CC], F32)

    def vcol(blk, cc):
        return vecs[:, blk * CC + cc:blk * CC + cc + 1]

    def acol(blk, cc):
        return aux[:, blk * CC + cc:blk * CC + cc + 1]

    def a2col(blk, cc):
        return aux2[:, blk * CC + cc:blk * CC + cc + 1]

    TM = {"k": 2, "v": 3, "r": 4, "g": 5}
    OMT = {"k": ("aux", 1), "v": ("aux", 2), "r": ("aux2", 0),
           "g": ("aux2", 1)}

    def omtcol(p, cc):
        t, blk = OMT[p]
        return acol(blk, cc) if t == "aux" else a2col(blk, cc)

    def xc(buf, cc):
        return buf[:, cc * TT:(cc + 1) * TT]

    P.op("sync", lambda e: e.dma_start(out=vecs[:], in_=vecs_in),
         writes=["vecs"], sem="cld", inc=16, dma=True)
    P.op("sync", lambda e: e.dma_start(out=h64f[:], in_=h64_in),
         writes=["h64f"], sem="cld", inc=16, dma=True)
    P.op("sync", lambda e: e.dma_start(out=h128f[:], in_=h128_in),
         writes=["h128f"], sem="cld", inc=16, dma=True)
    P.op("vector", lambda e: e.memset(onesb[:], 1.0), writes=["onesb"])
    P.op("vector", lambda e: e.memset(ones1[:], 1.0), writes=["ones1"])
    P.op("vector", lambda e: e.memset(epsr[:], EPS), writes=["epsr"])
    P.op("vector", lambda e: e.memset(acar[:], 0.0), writes=["acar"])
    P.op("vector", lambda e: e.memset(bcar[:], 0.0), writes=["bcar"])
    P.op("scalar", lambda e: e.copy(out=h128b[:], in_=h128f[:]),
         reads=["h128f"], writes=["h128b"])
    P.op("scalar", lambda e: e.activation(
            out=TP[0][:, 0:CC], in_=vecs[:, 0:CC], func=AF.Exp),
         reads=["vecs"], writes=["T0"])
    P.op("scalar", lambda e: e.activation(
            out=aux[:, 0:CC], in_=TP[0][:, 0:CC], func=AF.Exp, scale=-1.0),
         reads=["T0"], writes=["aux"])
    P.op("scalar", lambda e: e.activation(
            out=aux3[:, 0:CC], in_=vecs[:, CC:2 * CC], func=AF.Exp),
         reads=["vecs"], writes=["aux3"])
    for p in ["k", "v", "r", "g"]:
        t, blk = OMT[p]
        dst = aux if t == "aux" else aux2
        off = blk * CC
        P.op("vector", lambda e, dst=dst, off=off, p=p: e.tensor_scalar(
                out=dst[:, off:off + CC], in0=vecs[:, TM[p] * CC:(TM[p] + 1) * CC],
                scalar1=-1.0, scalar2=1.0, op0=OP.mult, op1=OP.add),
             reads=["vecs"], writes=[t])

    wcat_r = wcat.rearrange("(u p) f -> u p f", p=128)
    w_bf_r = w_bf.rearrange("(u p) f -> u p f", p=128)
    rp_state = {"seq": 0}

    w2t_r = w2t.rearrange("(hc p) co -> hc p co", p=128)
    w2s_r = w2s.rearrange("(hc p) co -> hc p co", p=128)
    n_ch = C // 1024

    tf_state = {"i": 0}
    fring = {"i": 0}

    def stage_f32(srcap):
        fsl = fring["i"] % 3
        fring["i"] += 1
        for h2, feng in ((0, "sync"), (1, "gpsimd")):
            P.op(feng, lambda e, srcap=srcap, fsl=fsl, h2=h2: e.dma_start(
                    out=rpf[fsl][:, h2 * 512:(h2 + 1) * 512],
                    in_=srcap[:, h2 * 512:(h2 + 1) * 512]),
                 writes=[f"rpf{fsl}"], sem=f"rpl{fsl}", inc=16, dma=True)
        return fsl

    def emit_transform_pieces(n):
        w2tp = w2t.rearrange("(i p) f -> i p f", p=128)
        for _ in range(n):
            i = tf_state["i"]
            if i >= 64 * n_ch:
                return
            tf_state["i"] += 1
            hpp, ch = divmod(i, n_ch)
            hp0, hp1 = 2 * hpp, 2 * hpp + 1
            sl = i % 2
            for h2, feng in ((0, "gpsimd"), (1, "sync")):
                P.op(feng, lambda e, i=i, sl=sl, h2=h2, w2tp=w2tp:
                     e.dma_start(
                        out=wtf[sl][:, h2 * 512:(h2 + 1) * 512],
                        in_=w2tp[i][:, h2 * 512:(h2 + 1) * 512]),
                     writes=[f"wtf{sl}"], sem=f"wtl{sl}", inc=16, dma=True)
            for s2 in range(2):
                bk = 4 + s2
                o0 = sl * 1024 + s2 * 512
                P.op("tensor", lambda e, sl=sl, s2=s2, bk=bk: e.matmul(
                        pb[bk][:, :], h64f[:, :],
                        wtf[sl][:, s2 * 512:(s2 + 1) * 512],
                        start=True, stop=True),
                     reads=[f"wtf{sl}", "h64f"], writes=[f"pb{bk}"])
                P.op("vector", lambda e, o0=o0, bk=bk, sl=sl: e.tensor_scalar(
                        out=w2o2[:, o0:o0 + 512],
                        in0=pb[bk][:, :], scalar1=SCL, scalar2=None,
                        op0=OP.mult),
                     reads=[f"pb{bk}"], writes=[f"w2o2{sl}"])
            P.op("sync", lambda e, hp0=hp0, ch=ch, sl=sl: e.dma_start(
                    out=w2s_r[:, hp0, ch * 1024:(ch + 1) * 1024],
                    in_=w2o2[0:HC, sl * 1024:(sl + 1) * 1024]),
                 reads=[f"w2o2{sl}"], writes=[f"w2s_{ch}"], sem="w2st",
                 inc=16, dma=True)
            P.op("sync", lambda e, hp1=hp1, ch=ch, sl=sl: e.dma_start(
                    out=w2s_r[:, hp1, ch * 1024:(ch + 1) * 1024],
                    in_=w2o2[HC:2 * HC, sl * 1024:(sl + 1) * 1024]),
                 reads=[f"w2o2{sl}"], writes=[f"w2s_{ch}"], sem="w2st",
                 inc=16, dma=True)

    ring = {"seq": 0}
    first_use = set()
    rp_seq = {"i": 0}

    def fill_slot(units):
        sl = ring["seq"] % NWR
        ring["seq"] += 1
        name = f"wr{sl}"
        ch = f"cw{sl}"
        u0 = units[0]
        if (isinstance(u0, int) and u0 not in first_use
                and all(isinstance(u, int) for u in units)):
            assert len(units) == 2 and units[1] == u0 + 1
            first_use.update(units)
            for j, u in enumerate(units):
                fsl = stage_f32(wcat_r[u])
                P.op("vector", lambda e, sl=sl, fsl=fsl, j=j: e.tensor_copy(
                        out=wr[sl][:, j * 1024:(j + 1) * 1024],
                        in_=rpf[fsl][:]),
                     reads=[f"rpf{fsl}"], writes=[name])
            wdst = w_bf.rearrange("(u p) f -> p u f", p=128)[:, u0:u0 + 2, :]
            P.op("scalar", lambda e, sl=sl, wdst=wdst: e.dma_start(
                    out=wdst,
                    in_=wr[sl][:].rearrange("p (u f) -> p u f", f=1024)),
                 reads=[name], writes=[f"wbf_u{u0}", f"wbf_u{u0+1}"],
                 sem="rps", inc=16, dma=True)
            return sl, wr[sl]
        for j, u in enumerate(units):
            ieng = "sync" if j == 0 else "gpsimd"
            if isinstance(u, int):
                P.op(ieng, lambda e, u=u, j=j, sl=sl: e.dma_start(
                        out=wr[sl][:, j * 1024:(j + 1) * 1024],
                        in_=w_bf_r[u]),
                     reads=[f"wbf_u{u}"], writes=[name], sem=ch, inc=16,
                     dma=True)
            else:
                r0, c0 = u
                P.op(ieng, lambda e, r0=r0, c0=c0, j=j, sl=sl: e.dma_start(
                        out=wr[sl][:, j * 1024:(j + 1) * 1024],
                        in_=w2s[r0:r0 + 128, c0:c0 + 1024]),
                     reads=[f"w2s_{c0 // 1024}"],
                     writes=[name], sem=ch, inc=16, dma=True)
        return sl, wr[sl]

    PIDX = {"k": 0, "v": 1, "r": 2, "g": 3, "o": 4}

    def proj_unit(p, h, kc):
        return (PIDX[p] * NHALF + h) * CC + kc

    def ffn_unit(q, ug, kc):
        return n_proj_units + (q * (HCQ // GRP) + ug) * CC + kc

    def load_x(tile, dst, dnames):
        sq, ti = divmod(tile, TPS)
        col0 = sq * T + ti * TT
        srcap = x_fm.rearrange("(cc p) n -> p cc n", p=128)[:, :, col0:col0 + TT]
        xdst = dst[:].rearrange("p (cc n) -> p cc n", n=TT)
        P.op("scalar", lambda e, srcap=srcap, xdst=xdst: e.dma_start(
                out=xdst, in_=srcap),
             reads=[], writes=dnames, sem="xld", inc=16, dma=True)

    load_x(0, xbuf, [f"xbuf_{c}" for c in range(CC)])
    for tile in range(NTILES):
        sq, ti = divmod(tile, TPS)
        t0 = ti * TT
        col0 = sq * T + t0
        bpar = tile % 2
        xb, xbn = (xbuf, "xbuf") if tile % 2 == 0 else (P1, "P1")
        pp, ppn = (P1, "P1") if tile % 2 == 0 else (xbuf, "xbuf")
        Xn = [f"{xbn}_{c}" for c in range(CC)]

        for cc in range(CC):
            cb = (cc % 2) * 2
            if cc % 2 == 0:
                P.op("scalar", lambda e, xb=xb, cc=cc, cb=cb: e.copy(
                        out=cbf[cb][:], in_=xc(xb, cc)),
                     reads=[f"{xbn}_{cc}"], writes=[f"cbf{cb}"])
                P.op("vector", lambda e, xb=xb, cc=cc, cb=cb: e.tensor_tensor(
                        out=cbf[cb + 1][:], in0=xc(xb, cc), in1=xc(xb, cc),
                        op=OP.mult),
                     reads=[f"{xbn}_{cc}"], writes=[f"cbf{cb+1}"])
            else:
                P.op("vector", lambda e, xb=xb, cc=cc, cb=cb: e.tensor_copy(
                        out=cbf[cb][:], in_=xc(xb, cc)),
                     reads=[f"{xbn}_{cc}"], writes=[f"cbf{cb}"])
                P.op("scalar", lambda e, xb=xb, cc=cc, cb=cb: e.activation(
                        out=cbf[cb + 1][:], in_=xc(xb, cc), func=AF.Square),
                     reads=[f"{xbn}_{cc}"], writes=[f"cbf{cb+1}"])
            P.op("tensor", lambda e, cc=cc, cb=cb: e.matmul(
                    pb[6][0:1, 0:TT], onesb[:], cbf[cb][:],
                    start=(cc == 0), stop=(cc == CC - 1)),
                 reads=[f"cbf{cb}", "onesb"], writes=["pb6"],
                 inc=1 if cc == CC - 1 else 0)
            P.op("tensor", lambda e, cc=cc, cb=cb: e.matmul(
                    pb[7][0:1, 0:TT], onesb[:], cbf[cb + 1][:],
                    start=(cc == 0), stop=(cc == CC - 1)),
                 reads=[f"cbf{cb+1}", "onesb"], writes=["pb7"],
                 inc=1)
        P.op("vector", lambda e, xb=xb, pp=pp: e.tensor_scalar(
                out=rows[:, 0:TT], in0=pb[6][0:1, 0:TT], scalar1=1.0 / C,
                scalar2=None, op0=OP.mult),
             reads=["pb6"], writes=["P2_0"])
        P.op("vector", lambda e, xb=xb, pp=pp: e.tensor_tensor(
                out=rows[:, 2 * TT:3 * TT], in0=rows[:, 0:TT],
                in1=rows[:, 0:TT], op=OP.mult),
             reads=["P2_0"], writes=["P2_2"])
        P.op("vector", lambda e, xb=xb, pp=pp: e.scalar_tensor_tensor(
                out=rows[:, 2 * TT:3 * TT], in0=pb[7][0:1, 0:TT],
                scalar=1.0 / C, in1=rows[:, 2 * TT:3 * TT],
                op0=OP.mult, op1=OP.subtract),
             reads=["pb7", "P2_2"], writes=["P2_2"])
        P.op("scalar", lambda e, xb=xb, pp=pp: e.activation(
                out=rows[:, 2 * TT:3 * TT], in_=rows[:, 2 * TT:3 * TT],
                func=AF.Ln, bias=epsr[0:1, 0:1]),
             reads=["P2_2", "epsr"], writes=["P2_2"])
        P.op("scalar", lambda e, xb=xb, pp=pp: e.activation(
                out=rows[:, TT:2 * TT], in_=rows[:, 2 * TT:3 * TT],
                func=AF.Exp, scale=-0.5),
             reads=["P2_2"], writes=["P2_1"])
        P.op("tensor", lambda e, xb=xb, pp=pp: e.matmul(pb[4][:, 0:TT], ones1[:],
                                          rows[:, 0:TT], start=True, stop=True),
             reads=["P2_0", "ones1"], writes=["pb4"])
        P.op("tensor", lambda e, xb=xb, pp=pp: e.matmul(pb[5][:, 0:TT], ones1[:],
                                          rows[:, TT:2 * TT], start=True,
                                          stop=True),
             reads=["P2_1", "ones1"], writes=["pb5"])
        P.op("vector", lambda e: e.tensor_copy(out=mubc[:], in_=pb[4][:, 0:TT]),
             reads=["pb4"], writes=["mubc"])
        P.op("vector", lambda e: e.tensor_copy(out=rsdbc[:], in_=pb[5][:, 0:TT]),
             reads=["pb5"], writes=["rsdbc"])

        for cc in range(CC):
            slot = cc % 2
            P.op("vector", lambda e, xb=xb, cc=cc, slot=slot: e.tensor_tensor(
                    out=TP[slot][:, 0:TT], in0=xc(xb, cc), in1=mubc[:],
                    op=OP.subtract),
                 reads=[f"{xbn}_{cc}", "mubc"], writes=[f"T{slot}"])
            P.op("vector", lambda e, cc=cc, slot=slot: e.tensor_tensor(
                    out=xc(attnG, cc), in0=TP[slot][:, 0:TT],
                    in1=rsdbc[:], op=OP.mult),
                 reads=[f"T{slot}", "rsdbc"], writes=[f"xnbf_{cc}"])
            if t0 == 0:
                P.op("vector", lambda e, cc=cc, bpar=bpar: e.tensor_copy(
                        out=xnb[bpar][:, cc:cc + 1], in_=xc(attnG, cc)[:, 1:2]),
                     reads=[f"xnbf_{cc}"], writes=[f"xnb{bpar}"])
            P.op("vector", lambda e, cc=cc, bpar=bpar: e.tensor_copy(
                    out=xnb[1 - bpar][:, cc:cc + 1],
                    in_=xc(attnG, cc)[:, TT - 1:TT]),
                 reads=[f"xnbf_{cc}"], writes=[f"xnb{1-bpar}"])

        def build_operand(p, B, Bn):
            for cc in range(CC):
                slot = cc % 2
                tn = f"T{slot}"
                P.op("vector", lambda e, cc=cc, slot=slot, p=p:
                     e.tensor_scalar(
                        out=TP[slot][:, 0:TT], in0=xc(attnG, cc),
                        scalar1=omtcol(p, cc), scalar2=None, op0=OP.mult),
                     reads=[f"xnbf_{cc}", OMT[p][0]], writes=[tn])
                P.op("vector", lambda e, cc=cc, slot=slot, p=p:
                     e.scalar_tensor_tensor(
                        out=xc(B, cc)[:, 1:TT], in0=xc(attnG, cc)[:, 0:TT - 1],
                        scalar=vcol(TM[p], cc), in1=TP[slot][:, 1:TT],
                        op0=OP.mult, op1=OP.add),
                     reads=[f"xnbf_{cc}", tn], writes=[f"{Bn}_{cc}"])
                P.op("vector", lambda e, cc=cc, slot=slot, p=p, bpar=bpar:
                     e.scalar_tensor_tensor(
                        out=xc(B, cc)[:, 0:1], in0=xnb[bpar][:, cc:cc + 1],
                        scalar=vcol(TM[p], cc), in1=TP[slot][:, 0:1],
                        op0=OP.mult, op1=OP.add),
                     reads=[f"xnb{bpar}", tn], writes=[f"{Bn}_{cc}"])
                if tile == 0:
                    emit_transform_pieces(2)

        def proj_matmuls(p, B, Bn, evac):
            for h in range(NHALF):
                ccs = range(h * 8, min((h + 1) * 8, CC))
                for kc in range(CC):
                    if kc % 2 == 0:
                        sl, wrt = fill_slot([proj_unit(p, h, kc),
                                             proj_unit(p, h, kc + 1)])
                    uoff = (kc % 2) * 1024
                    for ci, cc in enumerate(ccs):
                        P.op("tensor",
                             lambda e, xb=xb, pp=pp, sl=sl, uoff=uoff, ci=ci, kc=kc, cc=cc:
                             e.matmul(
                                pb[ci][:, 0:TT],
                                wr[sl][:, uoff + ci * 128:uoff + (ci + 1) * 128],
                                xc(B, kc),
                                start=(kc == 0), stop=(kc == CC - 1)),
                             reads=[f"wr{sl}", f"{Bn}_{kc}"],
                             writes=[f"pb{ci}"],
                             inc=1 if (kc == CC - 1 or
                                       (kc % 2 == 1 and ci == len(ccs) - 1))
                             else 0)
                for ci, cc in enumerate(ccs):
                    evac(cc, ci)

        def ffn_up_group(q, ug):
            for kc in range(CC):
                if kc % 2 == 0:
                    sl, wrt = fill_slot([ffn_unit(q, ug, kc),
                                         ffn_unit(q, ug, kc + 1)])
                uoff = (kc % 2) * 1024
                for hj in range(GRP):
                    P.op("tensor",
                         lambda e, sl=sl, uoff=uoff, hj=hj, kc=kc:
                         e.matmul(
                            pb[2 * hj][:, 0:TT],
                            wr[sl][:, uoff + hj * 128:uoff + (hj + 1) * 128],
                            xc(attnG, kc), start=(kc == 0),
                            stop=(kc == CC - 1)),
                         reads=[f"wr{sl}", f"xnbf_{kc}"],
                         writes=[f"pb{2*hj}"],
                         inc=1 if kc == CC - 1 else 0)
                    P.op("tensor",
                         lambda e, sl=sl, uoff=uoff, hj=hj, kc=kc:
                         e.matmul(
                            pb[2 * hj + 1][:, 0:TT],
                            wr[sl][:, uoff + 512 + hj * 128:
                                   uoff + 512 + (hj + 1) * 128],
                            xc(attnG, kc), start=(kc == 0),
                            stop=(kc == CC - 1)),
                         reads=[f"wr{sl}", f"xnbf_{kc}"],
                         writes=[f"pb{2*hj+1}"],
                         inc=1 if (kc == CC - 1 or
                                   (kc % 2 == 1 and hj == GRP - 1))
                         else 0)
            for hj in range(GRP):
                j = ug * GRP + hj
                slot = hj % 2
                P.op("scalar", lambda e, hj=hj, slot=slot: e.activation(
                        out=TP[slot][:, 0:TT], in_=pb[2 * hj][:, 0:TT],
                        func=AF.Silu),
                     reads=[f"pb{2*hj}"], writes=[f"T{slot}"])
                P.op("vector", lambda e, hj=hj, j=j, slot=slot:
                     e.tensor_tensor(
                        out=xc(B2, j), in0=TP[slot][:, 0:TT],
                        in1=pb[2 * hj + 1][:, 0:TT], op=OP.mult),
                     reads=[f"T{slot}", f"pb{2*hj+1}"],
                     writes=[f"B2_{j}"])

        def ffn_h128(q):
            for j in range(HCQ):
                bk = j % 4
                P.op("tensor", lambda e, j=j, bk=bk: e.matmul(
                        pb[bk][:, 0:TT], h128b[:], xc(B2, j),
                        start=True, stop=True),
                     reads=["h128b", f"B2_{j}"], writes=[f"pb{bk}"])
                if j % 2 == 0:
                    P.op("scalar", lambda e, j=j, bk=bk: e.activation(
                            out=xc(B2, j), in_=pb[bk][:, 0:TT], func=AF.Copy,
                            scale=1.0),
                         reads=[f"pb{bk}"], writes=[f"B2_{j}"])
                else:
                    P.op("vector", lambda e, j=j, bk=bk: e.tensor_copy(
                            out=xc(B2, j), in_=pb[bk][:, 0:TT]),
                         reads=[f"pb{bk}"], writes=[f"B2_{j}"])

        def ffn_down(q):
            for h in range(NHALF):
                ccs = range(h * 8, min((h + 1) * 8, CC))
                for j in range(HCQ):
                    kc2 = q * HCQ + j
                    if j % 2 == 0:
                        sl, wrt = fill_slot([(kc2 * 128, h * 1024),
                                             ((kc2 + 1) * 128, h * 1024)])
                    uoff = (j % 2) * 1024
                    for ci, cc in enumerate(ccs):
                        P.op("tensor",
                             lambda e, sl=sl, uoff=uoff, ci=ci, j=j:
                             e.matmul(
                                pb[ci][:, 0:TT],
                                wr[sl][:, uoff + ci * 128:uoff + (ci + 1) * 128],
                                xc(B2, j), start=(j == 0),
                                stop=(j == HCQ - 1)),
                             reads=[f"wr{sl}", f"B2_{j}"], writes=[f"pb{ci}"],
                             inc=1 if (j == HCQ - 1 or
                                       (j % 2 == 1 and ci == len(ccs) - 1))
                             else 0)
                for ci, cc in enumerate(ccs):
                    P.op("vector", lambda e, xb=xb, cc=cc, ci=ci: e.tensor_tensor(
                            out=xc(xb, cc), in0=xc(xb, cc),
                            in1=pb[ci][:, 0:TT], op=OP.add),
                         reads=[f"{xbn}_{cc}", f"pb{ci}"], writes=[f"{xbn}_{cc}"])

        build_operand("k", B1, "B1")

        def evac_k(cc, bank):
            P.op("scalar", lambda e, pp=pp, cc=cc, bank=bank: e.copy(
                    out=xc(pp, cc), in_=pb[bank][:, 0:TT]),
                 reads=[f"pb{bank}"], writes=[f"{ppn}_{cc}"])
        proj_matmuls("k", B1, "B1", evac_k)

        build_operand("v", B2, "B2")

        def evac_v(cc, bank):
            P.op("scalar", lambda e, cc=cc, bank=bank: e.copy(
                    out=xc(P2, cc), in_=pb[bank][:, 0:TT]),
                 reads=[f"pb{bank}"], writes=[f"P2_{cc}"])
        proj_matmuls("v", B2, "B2", evac_v)

        build_operand("r", B1, "B1")

        if ti == 0:
            P.op("vector", lambda e: e.memset(acar[:], 0.0), writes=["acar"])
            P.op("vector", lambda e: e.memset(bcar[:], 0.0), writes=["bcar"])
        for cc in range(CC):
            P.op("scalar", lambda e, pp=pp, cc=cc: e.activation(
                    out=TP[2][:, 0:TT], in_=xc(pp, cc), func=AF.Exp),
                 reads=[f"{ppn}_{cc}"], writes=["T2"])
            P.op("vector", lambda e, cc=cc: e.tensor_tensor(
                    out=xc(P2, cc), in0=TP[2][:, 0:TT], in1=xc(P2, cc),
                    op=OP.mult),
                 reads=["T2", f"P2_{cc}"], writes=[f"P2_{cc}"])
            P.op("vector", lambda e, cc=cc: e.tensor_copy(
                    out=abuf[:, 0:1], in_=acar[:, cc:cc + 1]),
                 reads=["acar"], writes=["abuf"])
            P.op("vector", lambda e, cc=cc: e.tensor_tensor_scan(
                    out=abuf[:, 1:TT + 1],
                    data0=acol(0, cc).broadcast_to((128, TT)),
                    data1=xc(P2, cc), initial=acar[:, cc:cc + 1],
                    op0=OP.mult, op1=OP.add),
                 reads=["aux", f"P2_{cc}", "acar"], writes=["abuf"])
            P.op("vector", lambda e, cc=cc: e.tensor_copy(
                    out=acar[:, cc:cc + 1], in_=abuf[:, TT:TT + 1]),
                 reads=["abuf"], writes=["acar"])
            P.op("vector", lambda e, cc=cc: e.tensor_copy(
                    out=bbuf[:, 0:1], in_=bcar[:, cc:cc + 1]),
                 reads=["bcar"], writes=["bbuf"])
            P.op("vector", lambda e, cc=cc: e.tensor_tensor_scan(
                    out=bbuf[:, 1:TT + 1],
                    data0=acol(0, cc).broadcast_to((128, TT)),
                    data1=TP[2][:, 0:TT], initial=bcar[:, cc:cc + 1],
                    op0=OP.mult, op1=OP.add),
                 reads=["aux", "T2", "bcar"], writes=["bbuf"])
            P.op("vector", lambda e, cc=cc: e.tensor_copy(
                    out=bcar[:, cc:cc + 1], in_=bbuf[:, TT:TT + 1]),
                 reads=["bbuf"], writes=["bcar"])
            P.op("vector", lambda e, cc=cc: e.scalar_tensor_tensor(
                    out=abuf[:, 0:TT], in0=xc(P2, cc),
                    scalar=aux3[:, cc:cc + 1], in1=abuf[:, 0:TT],
                    op0=OP.mult, op1=OP.add),
                 reads=[f"P2_{cc}", "aux3", "abuf"], writes=["abuf"])
            P.op("vector", lambda e, cc=cc: e.scalar_tensor_tensor(
                    out=bbuf[:, 0:TT], in0=TP[2][:, 0:TT],
                    scalar=aux3[:, cc:cc + 1], in1=bbuf[:, 0:TT],
                    op0=OP.mult, op1=OP.add),
                 reads=["T2", "aux3", "bbuf"], writes=["bbuf"])
            P.op("scalar", lambda e, cc=cc: e.activation(
                    out=TP[2][:, 0:TT], in_=bbuf[:, 0:TT], func=AF.Ln),
                 reads=["bbuf"], writes=["T2"])
            P.op("scalar", lambda e, cc=cc: e.activation(
                    out=TP[2][:, 0:TT], in_=TP[2][:, 0:TT], func=AF.Exp,
                    scale=-1.0),
                 reads=["T2"], writes=["T2"])
            P.op("vector", lambda e, pp=pp, cc=cc: e.tensor_tensor(
                    out=xc(pp, cc), in0=abuf[:, 0:TT], in1=TP[2][:, 0:TT],
                    op=OP.mult),
                 reads=["abuf", "T2"], writes=[f"{ppn}_{cc}"])
            if cc % GRP == GRP - 1 and tile > 0:
                ffn_up_group(0, cc // GRP)
            if tile == 0:
                emit_transform_pieces(2)

        if tile > 0:
            ffn_h128(0)
            ffn_down(0)

        def evac_r(cc, bank):
            slot = cc % 2
            P.op("scalar", lambda e, cc=cc, bank=bank, slot=slot:
                 e.activation(out=TP[slot][:, 0:TT], in_=pb[bank][:, 0:TT],
                              func=AF.Sigmoid),
                 reads=[f"pb{bank}"], writes=[f"T{slot}"])
            P.op("vector", lambda e, pp=pp, cc=cc, slot=slot: e.tensor_tensor(
                    out=xc(pp, cc), in0=xc(pp, cc), in1=TP[slot][:, 0:TT],
                    op=OP.mult),
                 reads=[f"{ppn}_{cc}", f"T{slot}"], writes=[f"{ppn}_{cc}"])
        proj_matmuls("r", B1, "B1", evac_r)

        build_operand("g", B2, "B2")

        def evac_g(cc, bank):
            slot = cc % 2
            P.op("scalar", lambda e, cc=cc, bank=bank, slot=slot:
                 e.activation(out=TP[slot][:, 0:TT], in_=pb[bank][:, 0:TT],
                              func=AF.Sigmoid),
                 reads=[f"pb{bank}"], writes=[f"T{slot}"])
            P.op("vector", lambda e, pp=pp, cc=cc, slot=slot: e.tensor_tensor(
                    out=xc(B1, cc), in0=xc(pp, cc), in1=TP[slot][:, 0:TT],
                    op=OP.mult),
                 reads=[f"{ppn}_{cc}", f"T{slot}"], writes=[f"B1_{cc}"])
        proj_matmuls("g", B2, "B2", evac_g)

        def evac_o(cc, bank):
            P.op("scalar", lambda e, pp=pp, cc=cc, bank=bank: e.copy(
                    out=xc(pp, cc), in_=pb[bank][:, 0:TT]),
                 reads=[f"pb{bank}"], writes=[f"{ppn}_{cc}"])
        for h in range(NHALF):
            ccs = range(h * 8, min((h + 1) * 8, CC))
            for kc in range(CC):
                if kc % 2 == 0:
                    sl, wrt = fill_slot([proj_unit("o", h, kc),
                                         proj_unit("o", h, kc + 1)])
                uoff = (kc % 2) * 1024
                for ci, cc in enumerate(ccs):
                    P.op("tensor", lambda e, sl=sl, uoff=uoff, ci=ci, kc=kc:
                         e.matmul(
                            pb[ci][:, 0:TT],
                            wr[sl][:, uoff + ci * 128:uoff + (ci + 1) * 128],
                            xc(B1, kc), start=(kc == 0),
                            stop=(kc == CC - 1)),
                         reads=[f"wr{sl}", f"B1_{kc}"], writes=[f"pb{ci}"],
                         inc=1 if (kc == CC - 1 or
                                   (kc % 2 == 1 and ci == len(ccs) - 1))
                         else 0)
            for ci, cc in enumerate(ccs):
                evac_o(cc, ci)

        for cc in range(CC):
            cb = (cc % 2) * 2
            if cc % 2 == 0:
                P.op("scalar", lambda e, pp=pp, cc=cc, cb=cb: e.copy(
                        out=cbf[cb][:], in_=xc(pp, cc)),
                     reads=[f"{ppn}_{cc}"], writes=[f"cbf{cb}"])
                P.op("vector", lambda e, pp=pp, cc=cc, cb=cb: e.tensor_tensor(
                        out=cbf[cb + 1][:], in0=xc(pp, cc), in1=xc(pp, cc),
                        op=OP.mult),
                     reads=[f"{ppn}_{cc}"], writes=[f"cbf{cb+1}"])
            else:
                P.op("vector", lambda e, pp=pp, cc=cc, cb=cb: e.tensor_copy(
                        out=cbf[cb][:], in_=xc(pp, cc)),
                     reads=[f"{ppn}_{cc}"], writes=[f"cbf{cb}"])
                P.op("scalar", lambda e, pp=pp, cc=cc, cb=cb: e.activation(
                        out=cbf[cb + 1][:], in_=xc(pp, cc), func=AF.Square),
                     reads=[f"{ppn}_{cc}"], writes=[f"cbf{cb+1}"])
            P.op("tensor", lambda e, cc=cc, cb=cb: e.matmul(
                    pb[6][0:1, 0:TT], onesb[:], cbf[cb][:],
                    start=(cc == 0), stop=(cc == CC - 1)),
                 reads=[f"cbf{cb}", "onesb"], writes=["pb6"],
                 inc=1 if cc == CC - 1 else 0)
            P.op("tensor", lambda e, cc=cc, cb=cb: e.matmul(
                    pb[7][0:1, 0:TT], onesb[:], cbf[cb + 1][:],
                    start=(cc == 0), stop=(cc == CC - 1)),
                 reads=[f"cbf{cb+1}", "onesb"], writes=["pb7"],
                 inc=1)
        P.op("vector", lambda e: e.tensor_scalar(
                out=rows[:, 2 * TT:3 * TT], in0=pb[6][0:1, 0:TT],
                scalar1=1.0 / C, scalar2=None, op0=OP.mult),
             reads=["pb6"], writes=["P2_2"])
        P.op("vector", lambda e: e.tensor_tensor(
                out=rows[:, 0:TT], in0=rows[:, 2 * TT:3 * TT],
                in1=rows[:, 2 * TT:3 * TT], op=OP.mult),
             reads=["P2_2"], writes=["P2_0"])
        P.op("vector", lambda e: e.scalar_tensor_tensor(
                out=rows[:, 0:TT], in0=pb[7][0:1, 0:TT],
                scalar=1.0 / C, in1=rows[:, 0:TT],
                op0=OP.mult, op1=OP.subtract),
             reads=["pb7", "P2_0"], writes=["P2_0"])
        P.op("scalar", lambda e: e.activation(
                out=rows[:, 0:TT], in_=rows[:, 0:TT],
                func=AF.Ln, bias=epsr[0:1, 0:1]),
             reads=["P2_0", "epsr"], writes=["P2_0"])
        P.op("scalar", lambda e: e.activation(
                out=rows[:, 0:TT], in_=rows[:, 0:TT],
                func=AF.Exp, scale=-0.5),
             reads=["P2_0"], writes=["P2_0"])
        P.op("tensor", lambda e: e.matmul(pb[4][:, 0:TT], ones1[:],
                                          rows[:, 2 * TT:3 * TT], start=True,
                                          stop=True),
             reads=["P2_2", "ones1"], writes=["pb4"])
        P.op("tensor", lambda e: e.matmul(pb[5][:, 0:TT], ones1[:],
                                          rows[:, 0:TT], start=True,
                                          stop=True),
             reads=["P2_0", "ones1"], writes=["pb5"])
        P.op("vector", lambda e: e.tensor_copy(out=mubc[:], in_=pb[4][:, 0:TT]),
             reads=["pb4"], writes=["mubc"])
        P.op("vector", lambda e: e.tensor_copy(out=rsdbc[:], in_=pb[5][:, 0:TT]),
             reads=["pb5"], writes=["rsdbc"])
        for cc in range(CC):
            slot = cc % 2
            P.op("vector", lambda e, xb=xb, cc=cc: e.tensor_scalar(
                    out=xc(xb, cc), in0=xc(xb, cc), scalar1=vcol(7, cc),
                    scalar2=None, op0=OP.add),
                 reads=[f"{xbn}_{cc}", "vecs"], writes=[f"{xbn}_{cc}"])
            P.op("vector", lambda e, pp=pp, cc=cc, slot=slot: e.tensor_tensor(
                    out=TP[slot][:, 0:TT], in0=xc(pp, cc), in1=mubc[:],
                    op=OP.subtract),
                 reads=[f"{ppn}_{cc}", "mubc"], writes=[f"T{slot}"])
            P.op("vector", lambda e, cc=cc, slot=slot: e.tensor_tensor(
                    out=TP[slot][:, 0:TT], in0=TP[slot][:, 0:TT],
                    in1=rsdbc[:], op=OP.mult),
                 reads=[f"T{slot}", "rsdbc"], writes=[f"T{slot}"])
            P.op("vector", lambda e, xb=xb, cc=cc, slot=slot: e.scalar_tensor_tensor(
                    out=xc(xb, cc), in0=TP[slot][:, 0:TT],
                    scalar=vcol(6, cc), in1=xc(xb, cc),
                    op0=OP.mult, op1=OP.add),
                 reads=[f"T{slot}", "vecs", f"{xbn}_{cc}"], writes=[f"{xbn}_{cc}"])

        if tile + 1 < NTILES:
            load_x(tile + 1, pp, [f"{ppn}_{c}" for c in range(CC)])

        for q in range(0 if tile == 0 else 1, HQ):
            for ug in range(HCQ // GRP):
                ffn_up_group(q, ug)
            ffn_h128(q)
            ffn_down(q)

        dst = y_fm.rearrange("(cc p) n -> p cc n", p=128)[:, :, col0:col0 + TT]
        xsrc = xb[:].rearrange("p (cc n) -> p cc n", n=TT)
        P.op("gpsimd", lambda e, xb=xb, pp=pp, dst=dst, xsrc=xsrc: e.dma_start(out=dst, in_=xsrc),
             reads=Xn, writes=[], sem="xst", inc=16, dma=True)

    P.wait("sync", "xst", 16 * NTILES)
    P.build()
    meta = dict(C=C, H=H, T=T, SEQS=SEQS, TT=TT, n_units=n_units,
                n_proj_units=n_proj_units, NHALF=NHALF, CC=CC, HC=HC,
                HQ=HQ, HCQ=HCQ, GRP=GRP)
    return nc, meta



def pack_weights(Wr, Wk, Wv, Wg, Wo, W1, W3, meta):
    CC, NHALF, HQ, HCQ, GRP = (meta["CC"], meta["NHALF"], meta["HQ"],
                               meta["HCQ"], meta["GRP"])
    units = []
    for W in [Wk, Wv, Wr, Wg, Wo]:
        WT = np.ascontiguousarray(W.T)
        for h in range(NHALF):
            for kc in range(CC):
                u = WT[kc * 128:(kc + 1) * 128, h * 1024:(h + 1) * 1024]
                units.append(u)
    W1T = np.ascontiguousarray(W1.T)
    W3T = np.ascontiguousarray(W3.T)
    for q in range(HQ):
        for ug in range(HCQ // GRP):
            base = (q * HCQ + ug * GRP) * 128
            for kc in range(CC):
                u = np.concatenate(
                    [W1T[kc * 128:(kc + 1) * 128, base:base + 512],
                     W3T[kc * 128:(kc + 1) * 128, base:base + 512]], axis=1)
                units.append(u)
    assert len(units) == meta["n_units"]
    return np.ascontiguousarray(np.concatenate(units, axis=0))


def pack_w2t(W2, meta):
    HC = meta["HC"]
    C = W2.shape[0]
    n_ch = C // 1024
    W2T = np.ascontiguousarray(W2.T)
    v = W2T.reshape(HC, 64, 2, n_ch, 1024)
    v = v.transpose(1, 3, 2, 0, 4)
    return np.ascontiguousarray(v.reshape(64 * n_ch * 128, 1024))


def hadamard_blockdiag128(hc):
    h = hadamard(hc)
    d = np.zeros((128, 128), np.float32)
    d[:hc, :hc] = h
    if 2 * hc <= 128:
        d[hc:2 * hc, hc:2 * hc] = h
    return d


def hadamard(n):
    h = np.array([[1.0]], dtype=np.float32)
    while h.shape[0] < n:
        h = np.block([[h, h], [h, -h]])
    return h.astype(np.float32)


def pack_vecs(td, tf, tmk, tmv, tmr, tmg, lng, lnb, CC):
    cols = []
    for v in [td, tf, tmk, tmv, tmr, tmg, lng, lnb]:
        cols.append(np.asarray(v, np.float32).reshape(CC, 128).T)
    return np.ascontiguousarray(np.concatenate(cols, axis=1))



_CACHE = {}


def kernel(L0_attn_x, L0_attn_time_decay, L0_attn_time_first, L0_attn_tm_k,
           L0_attn_tm_v, L0_attn_tm_r, L0_attn_tm_g, L0_attn_Wr, L0_attn_Wk,
           L0_attn_Wv, L0_attn_Wg, L0_attn_Wo, L0_attn_ln_g, L0_attn_ln_b,
           L0_ffn_W1, L0_ffn_W2, L0_ffn_W3):
    x = np.asarray(L0_attn_x, np.float32)
    B, T, C = x.shape
    H = L0_ffn_W1.shape[0]
    n_cores = 8
    SEQS = B // n_cores

    if "prog" not in _CACHE:
        _CACHE["prog"] = build_program(C=C, H=H, T=T, SEQS=SEQS, TT=512)
    nc, meta = _CACHE["prog"]

    wcat = pack_weights(np.asarray(L0_attn_Wr, np.float32),
                        np.asarray(L0_attn_Wk, np.float32),
                        np.asarray(L0_attn_Wv, np.float32),
                        np.asarray(L0_attn_Wg, np.float32),
                        np.asarray(L0_attn_Wo, np.float32),
                        np.asarray(L0_ffn_W1, np.float32),
                        np.asarray(L0_ffn_W3, np.float32), meta)
    w2t = pack_w2t(np.asarray(L0_ffn_W2, np.float32), meta)
    vecs = pack_vecs(L0_attn_time_decay, L0_attn_time_first, L0_attn_tm_k,
                     L0_attn_tm_v, L0_attn_tm_r, L0_attn_tm_g, L0_attn_ln_g,
                     L0_attn_ln_b, meta["CC"])
    h64 = hadamard(meta["HC"])
    h64d = np.zeros((128, 128), np.float32)
    h64d[:64, :64] = h64
    h64d[64:, 64:] = h64
    h128 = hadamard(128)

    in_maps = []
    for c in range(n_cores):
        xs = x[c * SEQS:(c + 1) * SEQS]
        x_fm = np.ascontiguousarray(
            xs.transpose(2, 0, 1).reshape(C, SEQS * T))
        in_maps.append({"x_fm": x_fm, "wcat": wcat, "w2t": w2t,
                        "vecs": vecs, "h64": h64d, "h128": h128})

    res = run_bass_kernel_spmd(nc, in_maps, core_ids=list(range(n_cores)))
    out = np.empty((B, T, C), np.float32)
    for c in range(n_cores):
        y = np.asarray(res.results[c]["y_fm"], np.float32)
        out[c * SEQS:(c + 1) * SEQS] = (
            y.reshape(C, SEQS, T).transpose(1, 2, 0))
    return out


def _run(nc, ins, n_cores=1, sim=False, trace=False):
    if sim:
        from concourse.bass_interp import CoreSim
        from concourse.libnrt import get_pod_config
        nc.finalize()
        sim_ = CoreSim(nc, require_finite=False, require_nnan=False,
                       **get_pod_config())
        for k, v in ins[0].items():
            sim_.tensor(k)[:] = v
        sim_.simulate(check_with_hw=False)
        return [dict(_sim=sim_)]
    res = run_bass_kernel_spmd(nc, ins, core_ids=list(range(n_cores)),
                               trace=trace)
    if trace:
        return res
    return res.results


# version 18
# speedup vs baseline: 1.0862x; 1.0577x over previous
import contextlib
import math
import numpy as np
import ml_dtypes
import concourse.bass as bass
import concourse.mybir as mybir
from concourse.bass_utils import run_bass_kernel_spmd
from concourse.alu_op_type import AluOpType

F32 = mybir.dt.float32
BF16 = mybir.dt.bfloat16
AF = mybir.ActivationFunctionType
OP = AluOpType

ENGINES = ("sync", "scalar", "vector", "tensor", "gpsimd")


class Prog:
    def __init__(self, nc):
        self.nc = nc
        self.es = contextlib.ExitStack()
        self.sems = {}
        self.cnt = {}
        self.waited = {e: {} for e in ENGINES}
        self.ops = {e: [] for e in ENGINES}
        self.buf = {}
        self.dma_chans = set()

    def sem(self, name):
        if name not in self.sems:
            self.sems[name] = self.es.enter_context(self.nc.semaphore(name))
            self.cnt[name] = 0
        return self.sems[name]

    def op(self, eng, fn, reads=(), writes=(), sem=None, inc=1, dma=False):
        s = sem or eng
        self.sem(s)
        if dma:
            self.dma_chans.add(s)
        deps = {}

        def need(ds, dc):
            if ds in self.dma_chans:
                dc = self.cnt[ds]
            if deps.get(ds, 0) < dc:
                deps[ds] = dc

        for b in reads:
            st = self.buf.get(b)
            if st and st["w"]:
                need(*st["w"])
        for b in writes:
            st = self.buf.get(b)
            if st:
                if st["w"]:
                    need(*st["w"])
                for rs, rc in st["r"].items():
                    need(rs, rc)
        waits = []
        for ds, dc in sorted(deps.items()):
            if eng == "tensor" and ds == "tensor":
                continue
            if self.waited[eng].get(ds, 0) >= dc:
                continue
            waits.append((ds, dc))
            self.waited[eng][ds] = dc
        self.cnt[s] += inc
        c_rec = self.cnt[s] + (1 if inc == 0 else 0)
        for b in reads:
            st = self.buf.setdefault(b, {"w": None, "r": {}})
            st["r"][s] = max(st["r"].get(s, 0), c_rec)
        for b in writes:
            self.buf[b] = {"w": (s, c_rec), "r": {}}
        self.ops[eng].append((waits, fn, s, inc))
        return c_rec

    def tick(self, eng):
        s = eng
        self.sem(s)
        self.cnt[s] += 1
        sem = self.sems[s]
        self.ops[eng].append(
            ([], lambda e, sem=sem: e.sem_inc(sem, 1), None, 0))
        return self.cnt[s]

    def wait(self, eng, sem_name, count):
        self.sem(sem_name)
        self.ops[eng].append(([(sem_name, count)], None, None, 0))
        self.waited[eng][sem_name] = max(self.waited[eng].get(sem_name, 0),
                                         count)

    def build(self):
        for eng in ENGINES:
            for waits, _, _, _ in self.ops[eng]:
                for ws, wc in waits:
                    assert wc <= self.cnt[ws], \
                        f"wait {ws}>={wc} never reached (final {self.cnt[ws]})"
        sems = self.sems
        with self.nc.Block() as block:
            for eng in ENGINES:
                if not self.ops[eng]:
                    continue
                oplist = self.ops[eng]

                def body(engobj, oplist=oplist):
                    for waits, fn, s, inc in oplist:
                        for ws, wc in waits:
                            engobj.wait_ge(sems[ws], wc)
                        if fn is not None:
                            instr = fn(engobj)
                            if inc:
                                instr.then_inc(sems[s], inc)

                getattr(block, eng)(body)



def build_program(C=2048, H=8192, T=2048, SEQS=2, TT=512):
    CC = C // 128
    HC = H // 128
    HQ = 4
    HCQ = HC // HQ
    GRP = 4
    TPS = T // TT
    NTILES = SEQS * TPS
    N = SEQS * T
    NHALF = (CC + 7) // 8
    EPS = 1e-5
    SCL = 1.0 / math.sqrt(H)

    n_proj_units = 5 * NHALF * CC
    n_ffn_units = HQ * (HCQ // GRP) * CC
    n_units = n_proj_units + n_ffn_units

    nc = bass.Bass("TRN2", target_bir_lowering=False)
    P = Prog(nc)

    x_fm = nc.dram_tensor("x_fm", [C, N], F32, kind="ExternalInput").ap()
    wcat = nc.dram_tensor("wcat", [n_units * 128, 1024], F32,
                          kind="ExternalInput").ap()
    w2t = nc.dram_tensor("w2t", [64 * (C // 1024) * 128, 1024], F32,
                         kind="ExternalInput").ap()
    vecs_in = nc.dram_tensor("vecs", [128, CC * 8], F32,
                             kind="ExternalInput").ap()
    h64_in = nc.dram_tensor("h64", [128, 128], F32, kind="ExternalInput").ap()
    h128_in = nc.dram_tensor("h128", [128, 128], F32,
                             kind="ExternalInput").ap()
    y_fm = nc.dram_tensor("y_fm", [C, N], F32, kind="ExternalOutput").ap()
    w_bf = nc.dram_tensor("w_bf", [n_units * 128, 1024], BF16,
                          kind="Internal").ap()
    w2s = nc.dram_tensor("w2s", [H, C], BF16, kind="Internal").ap()

    es = P.es
    sb = {}

    def S(name, shape, dt=F32):
        sb[name] = es.enter_context(nc.sbuf_tensor(name, shape, dt))
        return sb[name]

    xbuf = S("xbuf", [128, CC * TT], F32)
    P1 = S("P1", [128, CC * TT], F32)
    P2 = S("P2", [128, CC * TT], F32)
    B1 = S("B1", [128, CC * TT], BF16)
    B2 = S("B2", [128, CC * TT], BF16)
    attnG = S("attnG", [128, CC * TT], BF16)
    TP = [S(f"T{i}", [128, TT + 4], F32) for i in range(3)]
    mubc = S("mubc", [128, TT], F32)
    rsdbc = S("rsdbc", [128, TT], F32)
    abuf = S("abuf", [128, TT + 1], F32)
    bbuf = S("bbuf", [128, TT + 1], F32)
    NWR = 4
    wr = [S(f"wr{i}", [128, 2048], BF16) for i in range(NWR)]
    rpf = [S(f"rpf{i}", [128, 1024], F32) for i in range(3)]
    wtf = [S(f"wtf{i}", [128, 1024], F32) for i in range(2)]
    w2o2 = S("w2o2", [128, 2048], BF16)
    vecs = S("vecsb", [128, CC * 8], F32)
    aux = S("aux", [128, CC * 4], F32)
    xnb = [S(f"xnb{i}", [128, CC], F32) for i in range(2)]
    acar = S("acar", [128, CC], F32)
    bcar = S("bcar", [128, CC], F32)
    h128b = S("h128b", [128, 128], BF16)
    onesb = S("onesb", [128, 1], BF16)
    ones1 = S("ones1", [1, 128], F32)
    h64f = S("h64f", [128, 128], F32)
    h128f = S("h128f", [128, 128], F32)
    epsr = S("epsr", [1, 1], F32)
    cbf = [S(f"cbf{i}", [128, TT], BF16) for i in range(4)]
    rows = P2[0:1, 0:3 * TT]
    pb = [es.enter_context(nc.psum_tensor(f"pb{i}", [128, 512], F32))
          for i in range(8)]

    aux2 = S("aux2", [128, CC * 2], F32)
    aux3 = S("aux3", [128, CC], F32)

    def vcol(blk, cc):
        return vecs[:, blk * CC + cc:blk * CC + cc + 1]

    def acol(blk, cc):
        return aux[:, blk * CC + cc:blk * CC + cc + 1]

    def a2col(blk, cc):
        return aux2[:, blk * CC + cc:blk * CC + cc + 1]

    TM = {"k": 2, "v": 3, "r": 4, "g": 5}
    OMT = {"k": ("aux", 1), "v": ("aux", 2), "r": ("aux2", 0),
           "g": ("aux2", 1)}

    def omtcol(p, cc):
        t, blk = OMT[p]
        return acol(blk, cc) if t == "aux" else a2col(blk, cc)

    def xc(buf, cc):
        return buf[:, cc * TT:(cc + 1) * TT]

    P.op("sync", lambda e: e.dma_start(out=vecs[:], in_=vecs_in),
         writes=["vecs"], sem="cld", inc=16, dma=True)
    P.op("sync", lambda e: e.dma_start(out=h64f[:], in_=h64_in),
         writes=["h64f"], sem="cld", inc=16, dma=True)
    P.op("sync", lambda e: e.dma_start(out=h128f[:], in_=h128_in),
         writes=["h128f"], sem="cld", inc=16, dma=True)
    P.op("vector", lambda e: e.memset(onesb[:], 1.0), writes=["onesb"])
    P.op("vector", lambda e: e.memset(ones1[:], 1.0), writes=["ones1"])
    P.op("vector", lambda e: e.memset(epsr[:], EPS), writes=["epsr"])
    P.op("vector", lambda e: e.memset(acar[:], 0.0), writes=["acar"])
    P.op("vector", lambda e: e.memset(bcar[:], 0.0), writes=["bcar"])
    P.op("scalar", lambda e: e.copy(out=h128b[:], in_=h128f[:]),
         reads=["h128f"], writes=["h128b"])
    P.op("scalar", lambda e: e.activation(
            out=TP[0][:, 0:CC], in_=vecs[:, 0:CC], func=AF.Exp),
         reads=["vecs"], writes=["T0"])
    P.op("scalar", lambda e: e.activation(
            out=aux[:, 0:CC], in_=TP[0][:, 0:CC], func=AF.Exp, scale=-1.0),
         reads=["T0"], writes=["aux"])
    P.op("scalar", lambda e: e.activation(
            out=aux3[:, 0:CC], in_=vecs[:, CC:2 * CC], func=AF.Exp),
         reads=["vecs"], writes=["aux3"])
    for p in ["k", "v", "r", "g"]:
        t, blk = OMT[p]
        dst = aux if t == "aux" else aux2
        off = blk * CC
        P.op("vector", lambda e, dst=dst, off=off, p=p: e.tensor_scalar(
                out=dst[:, off:off + CC], in0=vecs[:, TM[p] * CC:(TM[p] + 1) * CC],
                scalar1=-1.0, scalar2=1.0, op0=OP.mult, op1=OP.add),
             reads=["vecs"], writes=[t])

    wcat_r = wcat.rearrange("(u p) f -> u p f", p=128)
    w_bf_r = w_bf.rearrange("(u p) f -> u p f", p=128)
    rp_state = {"seq": 0}

    w2t_r = w2t.rearrange("(hc p) co -> hc p co", p=128)
    w2s_r = w2s.rearrange("(hc p) co -> hc p co", p=128)
    n_ch = C // 1024

    tf_state = {"i": 0}
    fring = {"i": 0}

    def stage_f32(srcap):
        fsl = fring["i"] % 3
        fring["i"] += 1
        for h2, feng in ((0, "sync"), (1, "gpsimd")):
            P.op(feng, lambda e, srcap=srcap, fsl=fsl, h2=h2: e.dma_start(
                    out=rpf[fsl][:, h2 * 512:(h2 + 1) * 512],
                    in_=srcap[:, h2 * 512:(h2 + 1) * 512]),
                 writes=[f"rpf{fsl}"], sem=f"rpl{fsl}", inc=16, dma=True)
        return fsl

    def emit_transform_pieces(n):
        w2tp = w2t.rearrange("(i p) f -> i p f", p=128)
        for _ in range(n):
            i = tf_state["i"]
            if i >= 64 * n_ch:
                return
            tf_state["i"] += 1
            hpp, ch = divmod(i, n_ch)
            hp0, hp1 = 2 * hpp, 2 * hpp + 1
            sl = i % 2
            P.op("gpsimd", lambda e, i=i, sl=sl, w2tp=w2tp: e.dma_start(
                    out=wtf[sl][:], in_=w2tp[i]),
                 writes=[f"wtf{sl}"], sem=f"wtl{sl}", inc=16, dma=True)
            for s2 in range(2):
                bk = 4 + s2
                o0 = sl * 1024 + s2 * 512
                P.op("tensor", lambda e, sl=sl, s2=s2, bk=bk: e.matmul(
                        pb[bk][:, :], h64f[:, :],
                        wtf[sl][:, s2 * 512:(s2 + 1) * 512],
                        start=True, stop=True),
                     reads=[f"wtf{sl}", "h64f"], writes=[f"pb{bk}"])
                P.op("vector", lambda e, o0=o0, bk=bk, sl=sl: e.tensor_scalar(
                        out=w2o2[:, o0:o0 + 512],
                        in0=pb[bk][:, :], scalar1=SCL, scalar2=None,
                        op0=OP.mult),
                     reads=[f"pb{bk}"], writes=[f"w2o2{sl}"])
            P.op("sync", lambda e, hp0=hp0, ch=ch, sl=sl: e.dma_start(
                    out=w2s_r[:, hp0, ch * 1024:(ch + 1) * 1024],
                    in_=w2o2[0:HC, sl * 1024:(sl + 1) * 1024]),
                 reads=[f"w2o2{sl}"], writes=[f"w2s_{ch}"], sem="w2st",
                 inc=16, dma=True)
            P.op("sync", lambda e, hp1=hp1, ch=ch, sl=sl: e.dma_start(
                    out=w2s_r[:, hp1, ch * 1024:(ch + 1) * 1024],
                    in_=w2o2[HC:2 * HC, sl * 1024:(sl + 1) * 1024]),
                 reads=[f"w2o2{sl}"], writes=[f"w2s_{ch}"], sem="w2st",
                 inc=16, dma=True)

    ring = {"seq": 0}
    first_use = set()
    rp_seq = {"i": 0}

    def fill_slot(units):
        sl = ring["seq"] % NWR
        ring["seq"] += 1
        name = f"wr{sl}"
        ch = f"cw{sl}"
        u0 = units[0]
        if (isinstance(u0, int) and u0 not in first_use
                and all(isinstance(u, int) for u in units)):
            assert len(units) == 2 and units[1] == u0 + 1
            first_use.update(units)
            for j, u in enumerate(units):
                fsl = stage_f32(wcat_r[u])
                P.op("vector", lambda e, sl=sl, fsl=fsl, j=j: e.tensor_copy(
                        out=wr[sl][:, j * 1024:(j + 1) * 1024],
                        in_=rpf[fsl][:]),
                     reads=[f"rpf{fsl}"], writes=[name])
            wdst = w_bf.rearrange("(u p) f -> p u f", p=128)[:, u0:u0 + 2, :]
            P.op("scalar", lambda e, sl=sl, wdst=wdst: e.dma_start(
                    out=wdst,
                    in_=wr[sl][:].rearrange("p (u f) -> p u f", f=1024)),
                 reads=[name], writes=[f"wbf_u{u0}", f"wbf_u{u0+1}"],
                 sem="rps", inc=16, dma=True)
            return sl, wr[sl]
        for j, u in enumerate(units):
            ieng = "sync" if j == 0 else "gpsimd"
            if isinstance(u, int):
                P.op(ieng, lambda e, u=u, j=j, sl=sl: e.dma_start(
                        out=wr[sl][:, j * 1024:(j + 1) * 1024],
                        in_=w_bf_r[u]),
                     reads=[f"wbf_u{u}"], writes=[name], sem=ch, inc=16,
                     dma=True)
            else:
                r0, c0 = u
                P.op(ieng, lambda e, r0=r0, c0=c0, j=j, sl=sl: e.dma_start(
                        out=wr[sl][:, j * 1024:(j + 1) * 1024],
                        in_=w2s[r0:r0 + 128, c0:c0 + 1024]),
                     reads=[f"w2s_{c0 // 1024}"],
                     writes=[name], sem=ch, inc=16, dma=True)
        return sl, wr[sl]

    PIDX = {"k": 0, "v": 1, "r": 2, "g": 3, "o": 4}

    def proj_unit(p, h, kc):
        return (PIDX[p] * NHALF + h) * CC + kc

    def ffn_unit(q, ug, kc):
        return n_proj_units + (q * (HCQ // GRP) + ug) * CC + kc

    def load_x(tile, dst, dnames):
        sq, ti = divmod(tile, TPS)
        col0 = sq * T + ti * TT
        srcap = x_fm.rearrange("(cc p) n -> p cc n", p=128)[:, :, col0:col0 + TT]
        xdst = dst[:].rearrange("p (cc n) -> p cc n", n=TT)
        P.op("scalar", lambda e, srcap=srcap, xdst=xdst: e.dma_start(
                out=xdst, in_=srcap),
             reads=[], writes=dnames, sem="xld", inc=16, dma=True)

    load_x(0, xbuf, [f"xbuf_{c}" for c in range(CC)])
    for tile in range(NTILES):
        sq, ti = divmod(tile, TPS)
        t0 = ti * TT
        col0 = sq * T + t0
        bpar = tile % 2
        xb, xbn = (xbuf, "xbuf") if tile % 2 == 0 else (P1, "P1")
        pp, ppn = (P1, "P1") if tile % 2 == 0 else (xbuf, "xbuf")
        Xn = [f"{xbn}_{c}" for c in range(CC)]

        for cc in range(CC):
            cb = (cc % 2) * 2
            if cc % 2 == 0:
                P.op("scalar", lambda e, xb=xb, cc=cc, cb=cb: e.copy(
                        out=cbf[cb][:], in_=xc(xb, cc)),
                     reads=[f"{xbn}_{cc}"], writes=[f"cbf{cb}"])
                P.op("vector", lambda e, xb=xb, cc=cc, cb=cb: e.tensor_tensor(
                        out=cbf[cb + 1][:], in0=xc(xb, cc), in1=xc(xb, cc),
                        op=OP.mult),
                     reads=[f"{xbn}_{cc}"], writes=[f"cbf{cb+1}"])
            else:
                P.op("vector", lambda e, xb=xb, cc=cc, cb=cb: e.tensor_copy(
                        out=cbf[cb][:], in_=xc(xb, cc)),
                     reads=[f"{xbn}_{cc}"], writes=[f"cbf{cb}"])
                P.op("scalar", lambda e, xb=xb, cc=cc, cb=cb: e.activation(
                        out=cbf[cb + 1][:], in_=xc(xb, cc), func=AF.Square),
                     reads=[f"{xbn}_{cc}"], writes=[f"cbf{cb+1}"])
            P.op("tensor", lambda e, cc=cc, cb=cb: e.matmul(
                    pb[6][0:1, 0:TT], onesb[:], cbf[cb][:],
                    start=(cc == 0), stop=(cc == CC - 1)),
                 reads=[f"cbf{cb}", "onesb"], writes=["pb6"],
                 inc=1 if cc == CC - 1 else 0)
            P.op("tensor", lambda e, cc=cc, cb=cb: e.matmul(
                    pb[7][0:1, 0:TT], onesb[:], cbf[cb + 1][:],
                    start=(cc == 0), stop=(cc == CC - 1)),
                 reads=[f"cbf{cb+1}", "onesb"], writes=["pb7"],
                 inc=1)
        P.op("vector", lambda e, xb=xb, pp=pp: e.tensor_scalar(
                out=rows[:, 0:TT], in0=pb[6][0:1, 0:TT], scalar1=1.0 / C,
                scalar2=None, op0=OP.mult),
             reads=["pb6"], writes=["P2_0"])
        P.op("vector", lambda e, xb=xb, pp=pp: e.tensor_tensor(
                out=rows[:, 2 * TT:3 * TT], in0=rows[:, 0:TT],
                in1=rows[:, 0:TT], op=OP.mult),
             reads=["P2_0"], writes=["P2_2"])
        P.op("vector", lambda e, xb=xb, pp=pp: e.scalar_tensor_tensor(
                out=rows[:, 2 * TT:3 * TT], in0=pb[7][0:1, 0:TT],
                scalar=1.0 / C, in1=rows[:, 2 * TT:3 * TT],
                op0=OP.mult, op1=OP.subtract),
             reads=["pb7", "P2_2"], writes=["P2_2"])
        P.op("scalar", lambda e, xb=xb, pp=pp: e.activation(
                out=rows[:, 2 * TT:3 * TT], in_=rows[:, 2 * TT:3 * TT],
                func=AF.Ln, bias=epsr[0:1, 0:1]),
             reads=["P2_2", "epsr"], writes=["P2_2"])
        P.op("scalar", lambda e, xb=xb, pp=pp: e.activation(
                out=rows[:, TT:2 * TT], in_=rows[:, 2 * TT:3 * TT],
                func=AF.Exp, scale=-0.5),
             reads=["P2_2"], writes=["P2_1"])
        P.op("tensor", lambda e, xb=xb, pp=pp: e.matmul(pb[4][:, 0:TT], ones1[:],
                                          rows[:, 0:TT], start=True, stop=True),
             reads=["P2_0", "ones1"], writes=["pb4"])
        P.op("tensor", lambda e, xb=xb, pp=pp: e.matmul(pb[5][:, 0:TT], ones1[:],
                                          rows[:, TT:2 * TT], start=True,
                                          stop=True),
             reads=["P2_1", "ones1"], writes=["pb5"])
        P.op("vector", lambda e: e.tensor_copy(out=mubc[:], in_=pb[4][:, 0:TT]),
             reads=["pb4"], writes=["mubc"])
        P.op("vector", lambda e: e.tensor_copy(out=rsdbc[:], in_=pb[5][:, 0:TT]),
             reads=["pb5"], writes=["rsdbc"])

        for cc in range(CC):
            slot = cc % 2
            P.op("vector", lambda e, xb=xb, cc=cc, slot=slot: e.tensor_tensor(
                    out=TP[slot][:, 0:TT], in0=xc(xb, cc), in1=mubc[:],
                    op=OP.subtract),
                 reads=[f"{xbn}_{cc}", "mubc"], writes=[f"T{slot}"])
            P.op("vector", lambda e, cc=cc, slot=slot: e.tensor_tensor(
                    out=xc(attnG, cc), in0=TP[slot][:, 0:TT],
                    in1=rsdbc[:], op=OP.mult),
                 reads=[f"T{slot}", "rsdbc"], writes=[f"xnbf_{cc}"])
            if t0 == 0:
                P.op("vector", lambda e, cc=cc, bpar=bpar: e.tensor_copy(
                        out=xnb[bpar][:, cc:cc + 1], in_=xc(attnG, cc)[:, 1:2]),
                     reads=[f"xnbf_{cc}"], writes=[f"xnb{bpar}"])
            P.op("vector", lambda e, cc=cc, bpar=bpar: e.tensor_copy(
                    out=xnb[1 - bpar][:, cc:cc + 1],
                    in_=xc(attnG, cc)[:, TT - 1:TT]),
                 reads=[f"xnbf_{cc}"], writes=[f"xnb{1-bpar}"])

        def build_operand(p, B, Bn):
            for cc in range(CC):
                slot = cc % 2
                tn = f"T{slot}"
                P.op("vector", lambda e, cc=cc, slot=slot, p=p:
                     e.tensor_scalar(
                        out=TP[slot][:, 0:TT], in0=xc(attnG, cc),
                        scalar1=omtcol(p, cc), scalar2=None, op0=OP.mult),
                     reads=[f"xnbf_{cc}", OMT[p][0]], writes=[tn])
                P.op("vector", lambda e, cc=cc, slot=slot, p=p:
                     e.scalar_tensor_tensor(
                        out=xc(B, cc)[:, 1:TT], in0=xc(attnG, cc)[:, 0:TT - 1],
                        scalar=vcol(TM[p], cc), in1=TP[slot][:, 1:TT],
                        op0=OP.mult, op1=OP.add),
                     reads=[f"xnbf_{cc}", tn], writes=[f"{Bn}_{cc}"])
                P.op("vector", lambda e, cc=cc, slot=slot, p=p, bpar=bpar:
                     e.scalar_tensor_tensor(
                        out=xc(B, cc)[:, 0:1], in0=xnb[bpar][:, cc:cc + 1],
                        scalar=vcol(TM[p], cc), in1=TP[slot][:, 0:1],
                        op0=OP.mult, op1=OP.add),
                     reads=[f"xnb{bpar}", tn], writes=[f"{Bn}_{cc}"])
                if tile == 0:
                    emit_transform_pieces(2)

        def proj_matmuls(p, B, Bn, evac):
            for h in range(NHALF):
                ccs = range(h * 8, min((h + 1) * 8, CC))
                for kc in range(CC):
                    if kc % 2 == 0:
                        sl, wrt = fill_slot([proj_unit(p, h, kc),
                                             proj_unit(p, h, kc + 1)])
                    uoff = (kc % 2) * 1024
                    for ci, cc in enumerate(ccs):
                        P.op("tensor",
                             lambda e, xb=xb, pp=pp, sl=sl, uoff=uoff, ci=ci, kc=kc, cc=cc:
                             e.matmul(
                                pb[ci][:, 0:TT],
                                wr[sl][:, uoff + ci * 128:uoff + (ci + 1) * 128],
                                xc(B, kc),
                                start=(kc == 0), stop=(kc == CC - 1)),
                             reads=[f"wr{sl}", f"{Bn}_{kc}"],
                             writes=[f"pb{ci}"],
                             inc=1 if (kc == CC - 1 or
                                       (kc % 2 == 1 and ci == len(ccs) - 1))
                             else 0)
                for ci, cc in enumerate(ccs):
                    evac(cc, ci)

        def ffn_up_group(q, ug):
            for kc in range(CC):
                if kc % 2 == 0:
                    sl, wrt = fill_slot([ffn_unit(q, ug, kc),
                                         ffn_unit(q, ug, kc + 1)])
                uoff = (kc % 2) * 1024
                for hj in range(GRP):
                    P.op("tensor",
                         lambda e, sl=sl, uoff=uoff, hj=hj, kc=kc:
                         e.matmul(
                            pb[2 * hj][:, 0:TT],
                            wr[sl][:, uoff + hj * 128:uoff + (hj + 1) * 128],
                            xc(attnG, kc), start=(kc == 0),
                            stop=(kc == CC - 1)),
                         reads=[f"wr{sl}", f"xnbf_{kc}"],
                         writes=[f"pb{2*hj}"],
                         inc=1 if kc == CC - 1 else 0)
                    P.op("tensor",
                         lambda e, sl=sl, uoff=uoff, hj=hj, kc=kc:
                         e.matmul(
                            pb[2 * hj + 1][:, 0:TT],
                            wr[sl][:, uoff + 512 + hj * 128:
                                   uoff + 512 + (hj + 1) * 128],
                            xc(attnG, kc), start=(kc == 0),
                            stop=(kc == CC - 1)),
                         reads=[f"wr{sl}", f"xnbf_{kc}"],
                         writes=[f"pb{2*hj+1}"],
                         inc=1 if (kc == CC - 1 or
                                   (kc % 2 == 1 and hj == GRP - 1))
                         else 0)
            for hj in range(GRP):
                j = ug * GRP + hj
                slot = hj % 2
                P.op("scalar", lambda e, hj=hj, slot=slot: e.activation(
                        out=TP[slot][:, 0:TT], in_=pb[2 * hj][:, 0:TT],
                        func=AF.Silu),
                     reads=[f"pb{2*hj}"], writes=[f"T{slot}"])
                P.op("vector", lambda e, hj=hj, j=j, slot=slot:
                     e.tensor_tensor(
                        out=xc(B2, j), in0=TP[slot][:, 0:TT],
                        in1=pb[2 * hj + 1][:, 0:TT], op=OP.mult),
                     reads=[f"T{slot}", f"pb{2*hj+1}"],
                     writes=[f"B2_{j}"])

        def ffn_h128(q):
            for j in range(HCQ):
                bk = j % 4
                P.op("tensor", lambda e, j=j, bk=bk: e.matmul(
                        pb[bk][:, 0:TT], h128b[:], xc(B2, j),
                        start=True, stop=True),
                     reads=["h128b", f"B2_{j}"], writes=[f"pb{bk}"])
                if j % 2 == 0:
                    P.op("scalar", lambda e, j=j, bk=bk: e.activation(
                            out=xc(B2, j), in_=pb[bk][:, 0:TT], func=AF.Copy,
                            scale=1.0),
                         reads=[f"pb{bk}"], writes=[f"B2_{j}"])
                else:
                    P.op("vector", lambda e, j=j, bk=bk: e.tensor_copy(
                            out=xc(B2, j), in_=pb[bk][:, 0:TT]),
                         reads=[f"pb{bk}"], writes=[f"B2_{j}"])

        def ffn_down(q):
            for h in range(NHALF):
                ccs = range(h * 8, min((h + 1) * 8, CC))
                for j in range(HCQ):
                    kc2 = q * HCQ + j
                    if j % 2 == 0:
                        sl, wrt = fill_slot([(kc2 * 128, h * 1024),
                                             ((kc2 + 1) * 128, h * 1024)])
                    uoff = (j % 2) * 1024
                    for ci, cc in enumerate(ccs):
                        P.op("tensor",
                             lambda e, sl=sl, uoff=uoff, ci=ci, j=j:
                             e.matmul(
                                pb[ci][:, 0:TT],
                                wr[sl][:, uoff + ci * 128:uoff + (ci + 1) * 128],
                                xc(B2, j), start=(j == 0),
                                stop=(j == HCQ - 1)),
                             reads=[f"wr{sl}", f"B2_{j}"], writes=[f"pb{ci}"],
                             inc=1 if (j == HCQ - 1 or
                                       (j % 2 == 1 and ci == len(ccs) - 1))
                             else 0)
                for ci, cc in enumerate(ccs):
                    P.op("vector", lambda e, xb=xb, cc=cc, ci=ci: e.tensor_tensor(
                            out=xc(xb, cc), in0=xc(xb, cc),
                            in1=pb[ci][:, 0:TT], op=OP.add),
                         reads=[f"{xbn}_{cc}", f"pb{ci}"], writes=[f"{xbn}_{cc}"])

        build_operand("k", B1, "B1")

        def evac_k(cc, bank):
            P.op("scalar", lambda e, pp=pp, cc=cc, bank=bank: e.copy(
                    out=xc(pp, cc), in_=pb[bank][:, 0:TT]),
                 reads=[f"pb{bank}"], writes=[f"{ppn}_{cc}"])
        proj_matmuls("k", B1, "B1", evac_k)

        build_operand("v", B2, "B2")

        def evac_v(cc, bank):
            P.op("scalar", lambda e, cc=cc, bank=bank: e.copy(
                    out=xc(P2, cc), in_=pb[bank][:, 0:TT]),
                 reads=[f"pb{bank}"], writes=[f"P2_{cc}"])
        proj_matmuls("v", B2, "B2", evac_v)

        build_operand("r", B1, "B1")

        if ti == 0:
            P.op("vector", lambda e: e.memset(acar[:], 0.0), writes=["acar"])
            P.op("vector", lambda e: e.memset(bcar[:], 0.0), writes=["bcar"])
        for cc in range(CC):
            P.op("scalar", lambda e, pp=pp, cc=cc: e.activation(
                    out=TP[2][:, 0:TT], in_=xc(pp, cc), func=AF.Exp),
                 reads=[f"{ppn}_{cc}"], writes=["T2"])
            P.op("vector", lambda e, cc=cc: e.tensor_tensor(
                    out=xc(P2, cc), in0=TP[2][:, 0:TT], in1=xc(P2, cc),
                    op=OP.mult),
                 reads=["T2", f"P2_{cc}"], writes=[f"P2_{cc}"])
            P.op("vector", lambda e, cc=cc: e.tensor_copy(
                    out=abuf[:, 0:1], in_=acar[:, cc:cc + 1]),
                 reads=["acar"], writes=["abuf"])
            P.op("vector", lambda e, cc=cc: e.tensor_tensor_scan(
                    out=abuf[:, 1:TT + 1],
                    data0=acol(0, cc).broadcast_to((128, TT)),
                    data1=xc(P2, cc), initial=acar[:, cc:cc + 1],
                    op0=OP.mult, op1=OP.add),
                 reads=["aux", f"P2_{cc}", "acar"], writes=["abuf"])
            P.op("vector", lambda e, cc=cc: e.tensor_copy(
                    out=acar[:, cc:cc + 1], in_=abuf[:, TT:TT + 1]),
                 reads=["abuf"], writes=["acar"])
            P.op("vector", lambda e, cc=cc: e.tensor_copy(
                    out=bbuf[:, 0:1], in_=bcar[:, cc:cc + 1]),
                 reads=["bcar"], writes=["bbuf"])
            P.op("vector", lambda e, cc=cc: e.tensor_tensor_scan(
                    out=bbuf[:, 1:TT + 1],
                    data0=acol(0, cc).broadcast_to((128, TT)),
                    data1=TP[2][:, 0:TT], initial=bcar[:, cc:cc + 1],
                    op0=OP.mult, op1=OP.add),
                 reads=["aux", "T2", "bcar"], writes=["bbuf"])
            P.op("vector", lambda e, cc=cc: e.tensor_copy(
                    out=bcar[:, cc:cc + 1], in_=bbuf[:, TT:TT + 1]),
                 reads=["bbuf"], writes=["bcar"])
            P.op("vector", lambda e, cc=cc: e.scalar_tensor_tensor(
                    out=abuf[:, 0:TT], in0=xc(P2, cc),
                    scalar=aux3[:, cc:cc + 1], in1=abuf[:, 0:TT],
                    op0=OP.mult, op1=OP.add),
                 reads=[f"P2_{cc}", "aux3", "abuf"], writes=["abuf"])
            P.op("vector", lambda e, cc=cc: e.scalar_tensor_tensor(
                    out=bbuf[:, 0:TT], in0=TP[2][:, 0:TT],
                    scalar=aux3[:, cc:cc + 1], in1=bbuf[:, 0:TT],
                    op0=OP.mult, op1=OP.add),
                 reads=["T2", "aux3", "bbuf"], writes=["bbuf"])
            P.op("scalar", lambda e, cc=cc: e.activation(
                    out=TP[2][:, 0:TT], in_=bbuf[:, 0:TT], func=AF.Ln),
                 reads=["bbuf"], writes=["T2"])
            P.op("scalar", lambda e, cc=cc: e.activation(
                    out=TP[2][:, 0:TT], in_=TP[2][:, 0:TT], func=AF.Exp,
                    scale=-1.0),
                 reads=["T2"], writes=["T2"])
            P.op("vector", lambda e, pp=pp, cc=cc: e.tensor_tensor(
                    out=xc(pp, cc), in0=abuf[:, 0:TT], in1=TP[2][:, 0:TT],
                    op=OP.mult),
                 reads=["abuf", "T2"], writes=[f"{ppn}_{cc}"])
            if cc % GRP == GRP - 1 and tile > 0:
                ffn_up_group(0, cc // GRP)
            if tile == 0:
                emit_transform_pieces(2)

        if tile > 0:
            ffn_h128(0)
            ffn_down(0)

        def evac_r(cc, bank):
            slot = cc % 2
            P.op("scalar", lambda e, cc=cc, bank=bank, slot=slot:
                 e.activation(out=TP[slot][:, 0:TT], in_=pb[bank][:, 0:TT],
                              func=AF.Sigmoid),
                 reads=[f"pb{bank}"], writes=[f"T{slot}"])
            P.op("vector", lambda e, pp=pp, cc=cc, slot=slot: e.tensor_tensor(
                    out=xc(pp, cc), in0=xc(pp, cc), in1=TP[slot][:, 0:TT],
                    op=OP.mult),
                 reads=[f"{ppn}_{cc}", f"T{slot}"], writes=[f"{ppn}_{cc}"])
        proj_matmuls("r", B1, "B1", evac_r)

        build_operand("g", B2, "B2")

        def evac_g(cc, bank):
            slot = cc % 2
            P.op("scalar", lambda e, cc=cc, bank=bank, slot=slot:
                 e.activation(out=TP[slot][:, 0:TT], in_=pb[bank][:, 0:TT],
                              func=AF.Sigmoid),
                 reads=[f"pb{bank}"], writes=[f"T{slot}"])
            P.op("vector", lambda e, pp=pp, cc=cc, slot=slot: e.tensor_tensor(
                    out=xc(B1, cc), in0=xc(pp, cc), in1=TP[slot][:, 0:TT],
                    op=OP.mult),
                 reads=[f"{ppn}_{cc}", f"T{slot}"], writes=[f"B1_{cc}"])
        proj_matmuls("g", B2, "B2", evac_g)

        def evac_o(cc, bank):
            P.op("scalar", lambda e, pp=pp, cc=cc, bank=bank: e.copy(
                    out=xc(pp, cc), in_=pb[bank][:, 0:TT]),
                 reads=[f"pb{bank}"], writes=[f"{ppn}_{cc}"])
        for h in range(NHALF):
            ccs = range(h * 8, min((h + 1) * 8, CC))
            for kc in range(CC):
                if kc % 2 == 0:
                    sl, wrt = fill_slot([proj_unit("o", h, kc),
                                         proj_unit("o", h, kc + 1)])
                uoff = (kc % 2) * 1024
                for ci, cc in enumerate(ccs):
                    P.op("tensor", lambda e, sl=sl, uoff=uoff, ci=ci, kc=kc:
                         e.matmul(
                            pb[ci][:, 0:TT],
                            wr[sl][:, uoff + ci * 128:uoff + (ci + 1) * 128],
                            xc(B1, kc), start=(kc == 0),
                            stop=(kc == CC - 1)),
                         reads=[f"wr{sl}", f"B1_{kc}"], writes=[f"pb{ci}"],
                         inc=1 if (kc == CC - 1 or
                                   (kc % 2 == 1 and ci == len(ccs) - 1))
                         else 0)
            for ci, cc in enumerate(ccs):
                evac_o(cc, ci)

        for cc in range(CC):
            cb = (cc % 2) * 2
            if cc % 2 == 0:
                P.op("scalar", lambda e, pp=pp, cc=cc, cb=cb: e.copy(
                        out=cbf[cb][:], in_=xc(pp, cc)),
                     reads=[f"{ppn}_{cc}"], writes=[f"cbf{cb}"])
                P.op("vector", lambda e, pp=pp, cc=cc, cb=cb: e.tensor_tensor(
                        out=cbf[cb + 1][:], in0=xc(pp, cc), in1=xc(pp, cc),
                        op=OP.mult),
                     reads=[f"{ppn}_{cc}"], writes=[f"cbf{cb+1}"])
            else:
                P.op("vector", lambda e, pp=pp, cc=cc, cb=cb: e.tensor_copy(
                        out=cbf[cb][:], in_=xc(pp, cc)),
                     reads=[f"{ppn}_{cc}"], writes=[f"cbf{cb}"])
                P.op("scalar", lambda e, pp=pp, cc=cc, cb=cb: e.activation(
                        out=cbf[cb + 1][:], in_=xc(pp, cc), func=AF.Square),
                     reads=[f"{ppn}_{cc}"], writes=[f"cbf{cb+1}"])
            P.op("tensor", lambda e, cc=cc, cb=cb: e.matmul(
                    pb[6][0:1, 0:TT], onesb[:], cbf[cb][:],
                    start=(cc == 0), stop=(cc == CC - 1)),
                 reads=[f"cbf{cb}", "onesb"], writes=["pb6"],
                 inc=1 if cc == CC - 1 else 0)
            P.op("tensor", lambda e, cc=cc, cb=cb: e.matmul(
                    pb[7][0:1, 0:TT], onesb[:], cbf[cb + 1][:],
                    start=(cc == 0), stop=(cc == CC - 1)),
                 reads=[f"cbf{cb+1}", "onesb"], writes=["pb7"],
                 inc=1)
        P.op("vector", lambda e: e.tensor_scalar(
                out=rows[:, 2 * TT:3 * TT], in0=pb[6][0:1, 0:TT],
                scalar1=1.0 / C, scalar2=None, op0=OP.mult),
             reads=["pb6"], writes=["P2_2"])
        P.op("vector", lambda e: e.tensor_tensor(
                out=rows[:, 0:TT], in0=rows[:, 2 * TT:3 * TT],
                in1=rows[:, 2 * TT:3 * TT], op=OP.mult),
             reads=["P2_2"], writes=["P2_0"])
        P.op("vector", lambda e: e.scalar_tensor_tensor(
                out=rows[:, 0:TT], in0=pb[7][0:1, 0:TT],
                scalar=1.0 / C, in1=rows[:, 0:TT],
                op0=OP.mult, op1=OP.subtract),
             reads=["pb7", "P2_0"], writes=["P2_0"])
        P.op("scalar", lambda e: e.activation(
                out=rows[:, 0:TT], in_=rows[:, 0:TT],
                func=AF.Ln, bias=epsr[0:1, 0:1]),
             reads=["P2_0", "epsr"], writes=["P2_0"])
        P.op("scalar", lambda e: e.activation(
                out=rows[:, 0:TT], in_=rows[:, 0:TT],
                func=AF.Exp, scale=-0.5),
             reads=["P2_0"], writes=["P2_0"])
        P.op("tensor", lambda e: e.matmul(pb[4][:, 0:TT], ones1[:],
                                          rows[:, 2 * TT:3 * TT], start=True,
                                          stop=True),
             reads=["P2_2", "ones1"], writes=["pb4"])
        P.op("tensor", lambda e: e.matmul(pb[5][:, 0:TT], ones1[:],
                                          rows[:, 0:TT], start=True,
                                          stop=True),
             reads=["P2_0", "ones1"], writes=["pb5"])
        P.op("vector", lambda e: e.tensor_copy(out=mubc[:], in_=pb[4][:, 0:TT]),
             reads=["pb4"], writes=["mubc"])
        P.op("vector", lambda e: e.tensor_copy(out=rsdbc[:], in_=pb[5][:, 0:TT]),
             reads=["pb5"], writes=["rsdbc"])
        for cc in range(CC):
            slot = cc % 2
            P.op("vector", lambda e, xb=xb, cc=cc: e.tensor_scalar(
                    out=xc(xb, cc), in0=xc(xb, cc), scalar1=vcol(7, cc),
                    scalar2=None, op0=OP.add),
                 reads=[f"{xbn}_{cc}", "vecs"], writes=[f"{xbn}_{cc}"])
            P.op("vector", lambda e, pp=pp, cc=cc, slot=slot: e.tensor_tensor(
                    out=TP[slot][:, 0:TT], in0=xc(pp, cc), in1=mubc[:],
                    op=OP.subtract),
                 reads=[f"{ppn}_{cc}", "mubc"], writes=[f"T{slot}"])
            P.op("vector", lambda e, cc=cc, slot=slot: e.tensor_tensor(
                    out=TP[slot][:, 0:TT], in0=TP[slot][:, 0:TT],
                    in1=rsdbc[:], op=OP.mult),
                 reads=[f"T{slot}", "rsdbc"], writes=[f"T{slot}"])
            P.op("vector", lambda e, xb=xb, cc=cc, slot=slot: e.scalar_tensor_tensor(
                    out=xc(xb, cc), in0=TP[slot][:, 0:TT],
                    scalar=vcol(6, cc), in1=xc(xb, cc),
                    op0=OP.mult, op1=OP.add),
                 reads=[f"T{slot}", "vecs", f"{xbn}_{cc}"], writes=[f"{xbn}_{cc}"])

        if tile + 1 < NTILES:
            load_x(tile + 1, pp, [f"{ppn}_{c}" for c in range(CC)])

        for q in range(0 if tile == 0 else 1, HQ):
            for ug in range(HCQ // GRP):
                ffn_up_group(q, ug)
            ffn_h128(q)
            ffn_down(q)

        dst = y_fm.rearrange("(cc p) n -> p cc n", p=128)[:, :, col0:col0 + TT]
        xsrc = xb[:].rearrange("p (cc n) -> p cc n", n=TT)
        P.op("gpsimd", lambda e, xb=xb, pp=pp, dst=dst, xsrc=xsrc: e.dma_start(out=dst, in_=xsrc),
             reads=Xn, writes=[], sem="xst", inc=16, dma=True)

    P.wait("sync", "xst", 16 * NTILES)
    P.build()
    meta = dict(C=C, H=H, T=T, SEQS=SEQS, TT=TT, n_units=n_units,
                n_proj_units=n_proj_units, NHALF=NHALF, CC=CC, HC=HC,
                HQ=HQ, HCQ=HCQ, GRP=GRP)
    return nc, meta



def pack_weights(Wr, Wk, Wv, Wg, Wo, W1, W3, meta):
    CC, NHALF, HQ, HCQ, GRP = (meta["CC"], meta["NHALF"], meta["HQ"],
                               meta["HCQ"], meta["GRP"])
    units = []
    for W in [Wk, Wv, Wr, Wg, Wo]:
        WT = np.ascontiguousarray(W.T)
        for h in range(NHALF):
            for kc in range(CC):
                u = WT[kc * 128:(kc + 1) * 128, h * 1024:(h + 1) * 1024]
                units.append(u)
    W1T = np.ascontiguousarray(W1.T)
    W3T = np.ascontiguousarray(W3.T)
    for q in range(HQ):
        for ug in range(HCQ // GRP):
            base = (q * HCQ + ug * GRP) * 128
            for kc in range(CC):
                u = np.concatenate(
                    [W1T[kc * 128:(kc + 1) * 128, base:base + 512],
                     W3T[kc * 128:(kc + 1) * 128, base:base + 512]], axis=1)
                units.append(u)
    assert len(units) == meta["n_units"]
    return np.ascontiguousarray(np.concatenate(units, axis=0))


def pack_w2t(W2, meta):
    HC = meta["HC"]
    C = W2.shape[0]
    n_ch = C // 1024
    W2T = np.ascontiguousarray(W2.T)
    v = W2T.reshape(HC, 64, 2, n_ch, 1024)
    v = v.transpose(1, 3, 2, 0, 4)
    return np.ascontiguousarray(v.reshape(64 * n_ch * 128, 1024))


def hadamard_blockdiag128(hc):
    h = hadamard(hc)
    d = np.zeros((128, 128), np.float32)
    d[:hc, :hc] = h
    if 2 * hc <= 128:
        d[hc:2 * hc, hc:2 * hc] = h
    return d


def hadamard(n):
    h = np.array([[1.0]], dtype=np.float32)
    while h.shape[0] < n:
        h = np.block([[h, h], [h, -h]])
    return h.astype(np.float32)


def pack_vecs(td, tf, tmk, tmv, tmr, tmg, lng, lnb, CC):
    cols = []
    for v in [td, tf, tmk, tmv, tmr, tmg, lng, lnb]:
        cols.append(np.asarray(v, np.float32).reshape(CC, 128).T)
    return np.ascontiguousarray(np.concatenate(cols, axis=1))



_CACHE = {}


def kernel(L0_attn_x, L0_attn_time_decay, L0_attn_time_first, L0_attn_tm_k,
           L0_attn_tm_v, L0_attn_tm_r, L0_attn_tm_g, L0_attn_Wr, L0_attn_Wk,
           L0_attn_Wv, L0_attn_Wg, L0_attn_Wo, L0_attn_ln_g, L0_attn_ln_b,
           L0_ffn_W1, L0_ffn_W2, L0_ffn_W3):
    x = np.asarray(L0_attn_x, np.float32)
    B, T, C = x.shape
    H = L0_ffn_W1.shape[0]
    n_cores = 8
    SEQS = B // n_cores

    if "prog" not in _CACHE:
        _CACHE["prog"] = build_program(C=C, H=H, T=T, SEQS=SEQS, TT=512)
    nc, meta = _CACHE["prog"]

    wcat = pack_weights(np.asarray(L0_attn_Wr, np.float32),
                        np.asarray(L0_attn_Wk, np.float32),
                        np.asarray(L0_attn_Wv, np.float32),
                        np.asarray(L0_attn_Wg, np.float32),
                        np.asarray(L0_attn_Wo, np.float32),
                        np.asarray(L0_ffn_W1, np.float32),
                        np.asarray(L0_ffn_W3, np.float32), meta)
    w2t = pack_w2t(np.asarray(L0_ffn_W2, np.float32), meta)
    vecs = pack_vecs(L0_attn_time_decay, L0_attn_time_first, L0_attn_tm_k,
                     L0_attn_tm_v, L0_attn_tm_r, L0_attn_tm_g, L0_attn_ln_g,
                     L0_attn_ln_b, meta["CC"])
    h64 = hadamard(meta["HC"])
    h64d = np.zeros((128, 128), np.float32)
    h64d[:64, :64] = h64
    h64d[64:, 64:] = h64
    h128 = hadamard(128)

    in_maps = []
    for c in range(n_cores):
        xs = x[c * SEQS:(c + 1) * SEQS]
        x_fm = np.ascontiguousarray(
            xs.transpose(2, 0, 1).reshape(C, SEQS * T))
        in_maps.append({"x_fm": x_fm, "wcat": wcat, "w2t": w2t,
                        "vecs": vecs, "h64": h64d, "h128": h128})

    res = run_bass_kernel_spmd(nc, in_maps, core_ids=list(range(n_cores)))
    out = np.empty((B, T, C), np.float32)
    for c in range(n_cores):
        y = np.asarray(res.results[c]["y_fm"], np.float32)
        out[c * SEQS:(c + 1) * SEQS] = (
            y.reshape(C, SEQS, T).transpose(1, 2, 0))
    return out


def _run(nc, ins, n_cores=1, sim=False, trace=False):
    if sim:
        from concourse.bass_interp import CoreSim
        from concourse.libnrt import get_pod_config
        nc.finalize()
        sim_ = CoreSim(nc, require_finite=False, require_nnan=False,
                       **get_pod_config())
        for k, v in ins[0].items():
            sim_.tensor(k)[:] = v
        sim_.simulate(check_with_hw=False)
        return [dict(_sim=sim_)]
    res = run_bass_kernel_spmd(nc, ins, core_ids=list(range(n_cores)),
                               trace=trace)
    if trace:
        return res
    return res.results


# version 19 (final)
# speedup vs baseline: 1.1421x; 1.0514x over previous
import contextlib
import math
import numpy as np
import ml_dtypes
import concourse.bass as bass
import concourse.mybir as mybir
from concourse.bass_utils import run_bass_kernel_spmd
from concourse.alu_op_type import AluOpType

F32 = mybir.dt.float32
BF16 = mybir.dt.bfloat16
AF = mybir.ActivationFunctionType
OP = AluOpType

ENGINES = ("sync", "scalar", "vector", "tensor", "gpsimd")


class Prog:
    def __init__(self, nc):
        self.nc = nc
        self.es = contextlib.ExitStack()
        self.sems = {}
        self.cnt = {}
        self.waited = {e: {} for e in ENGINES}
        self.ops = {e: [] for e in ENGINES}
        self.buf = {}
        self.dma_chans = set()

    def sem(self, name):
        if name not in self.sems:
            self.sems[name] = self.es.enter_context(self.nc.semaphore(name))
            self.cnt[name] = 0
        return self.sems[name]

    def op(self, eng, fn, reads=(), writes=(), sem=None, inc=1, dma=False):
        s = sem or eng
        self.sem(s)
        if dma:
            self.dma_chans.add(s)
        deps = {}

        def need(ds, dc):
            if ds in self.dma_chans:
                dc = self.cnt[ds]
            if deps.get(ds, 0) < dc:
                deps[ds] = dc

        for b in reads:
            st = self.buf.get(b)
            if st and st["w"]:
                need(*st["w"])
        for b in writes:
            st = self.buf.get(b)
            if st:
                if st["w"]:
                    need(*st["w"])
                for rs, rc in st["r"].items():
                    need(rs, rc)
        waits = []
        for ds, dc in sorted(deps.items()):
            if eng == "tensor" and ds == "tensor":
                continue
            if self.waited[eng].get(ds, 0) >= dc:
                continue
            waits.append((ds, dc))
            self.waited[eng][ds] = dc
        self.cnt[s] += inc
        c_rec = self.cnt[s] + (1 if inc == 0 else 0)
        for b in reads:
            st = self.buf.setdefault(b, {"w": None, "r": {}})
            st["r"][s] = max(st["r"].get(s, 0), c_rec)
        for b in writes:
            self.buf[b] = {"w": (s, c_rec), "r": {}}
        self.ops[eng].append((waits, fn, s, inc))
        return c_rec

    def tick(self, eng):
        s = eng
        self.sem(s)
        self.cnt[s] += 1
        sem = self.sems[s]
        self.ops[eng].append(
            ([], lambda e, sem=sem: e.sem_inc(sem, 1), None, 0))
        return self.cnt[s]

    def wait(self, eng, sem_name, count):
        self.sem(sem_name)
        self.ops[eng].append(([(sem_name, count)], None, None, 0))
        self.waited[eng][sem_name] = max(self.waited[eng].get(sem_name, 0),
                                         count)

    def build(self):
        for eng in ENGINES:
            for waits, _, _, _ in self.ops[eng]:
                for ws, wc in waits:
                    assert wc <= self.cnt[ws], \
                        f"wait {ws}>={wc} never reached (final {self.cnt[ws]})"
        sems = self.sems
        with self.nc.Block() as block:
            for eng in ENGINES:
                if not self.ops[eng]:
                    continue
                oplist = self.ops[eng]

                def body(engobj, oplist=oplist):
                    for waits, fn, s, inc in oplist:
                        for ws, wc in waits:
                            engobj.wait_ge(sems[ws], wc)
                        if fn is not None:
                            instr = fn(engobj)
                            if inc:
                                instr.then_inc(sems[s], inc)

                getattr(block, eng)(body)



def build_program(C=2048, H=8192, T=2048, SEQS=2, TT=512):
    CC = C // 128
    HC = H // 128
    HQ = 4
    HCQ = HC // HQ
    GRP = 4
    TPS = T // TT
    NTILES = SEQS * TPS
    N = SEQS * T
    NHALF = (CC + 7) // 8
    EPS = 1e-5
    SCL = 1.0 / math.sqrt(H)

    n_proj_units = 5 * NHALF * CC
    n_ffn_units = HQ * (HCQ // GRP) * CC
    n_units = n_proj_units + n_ffn_units

    nc = bass.Bass("TRN2", target_bir_lowering=False)
    P = Prog(nc)

    x_fm = nc.dram_tensor("x_fm", [C, N], F32, kind="ExternalInput").ap()
    wcat = nc.dram_tensor("wcat", [n_units * 128, 1024], F32,
                          kind="ExternalInput").ap()
    w2t = nc.dram_tensor("w2t", [64 * (C // 1024) * 128, 1024], F32,
                         kind="ExternalInput").ap()
    vecs_in = nc.dram_tensor("vecs", [128, CC * 8], F32,
                             kind="ExternalInput").ap()
    h64_in = nc.dram_tensor("h64", [128, 128], F32, kind="ExternalInput").ap()
    h128_in = nc.dram_tensor("h128", [128, 128], F32,
                             kind="ExternalInput").ap()
    y_fm = nc.dram_tensor("y_fm", [C, N], F32, kind="ExternalOutput").ap()
    w_bf = nc.dram_tensor("w_bf", [n_units * 128, 1024], BF16,
                          kind="Internal").ap()
    w2s = nc.dram_tensor("w2s", [H, C], BF16, kind="Internal").ap()

    es = P.es
    sb = {}

    def S(name, shape, dt=F32):
        sb[name] = es.enter_context(nc.sbuf_tensor(name, shape, dt))
        return sb[name]

    xbuf = S("xbuf", [128, CC * TT], F32)
    P1 = S("P1", [128, CC * TT], F32)
    P2 = S("P2", [128, CC * TT], F32)
    B1 = S("B1", [128, CC * TT], BF16)
    B2 = S("B2", [128, CC * TT], BF16)
    attnG = S("attnG", [128, CC * TT], BF16)
    TP = [S(f"T{i}", [128, TT + 4], F32) for i in range(3)]
    mubc = S("mubc", [128, TT], F32)
    rsdbc = S("rsdbc", [128, TT], F32)
    abuf = S("abuf", [128, TT + 1], F32)
    bbuf = S("bbuf", [128, TT + 1], F32)
    NWR = 4
    wr = [S(f"wr{i}", [128, 2048], BF16) for i in range(NWR)]
    rpf = [S(f"rpf{i}", [128, 1024], F32) for i in range(3)]
    wtf = [S(f"wtf{i}", [128, 1024], F32) for i in range(2)]
    w2o2 = S("w2o2", [128, 2048], BF16)
    vecs = S("vecsb", [128, CC * 8], F32)
    aux = S("aux", [128, CC * 4], F32)
    xnb = [S(f"xnb{i}", [128, CC], F32) for i in range(2)]
    acar = S("acar", [128, CC], F32)
    bcar = S("bcar", [128, CC], F32)
    h128b = S("h128b", [128, 128], BF16)
    onesb = S("onesb", [128, 1], BF16)
    ones1 = S("ones1", [1, 128], F32)
    h64f = S("h64f", [128, 128], F32)
    h128f = S("h128f", [128, 128], F32)
    epsr = S("epsr", [1, 1], F32)
    cbf = [S(f"cbf{i}", [128, TT], BF16) for i in range(4)]
    rows = P2[0:1, 0:3 * TT]
    pb = [es.enter_context(nc.psum_tensor(f"pb{i}", [128, 512], F32))
          for i in range(8)]

    aux2 = S("aux2", [128, CC * 2], F32)
    aux3 = S("aux3", [128, CC], F32)

    def vcol(blk, cc):
        return vecs[:, blk * CC + cc:blk * CC + cc + 1]

    def acol(blk, cc):
        return aux[:, blk * CC + cc:blk * CC + cc + 1]

    def a2col(blk, cc):
        return aux2[:, blk * CC + cc:blk * CC + cc + 1]

    TM = {"k": 2, "v": 3, "r": 4, "g": 5}
    OMT = {"k": ("aux", 1), "v": ("aux", 2), "r": ("aux2", 0),
           "g": ("aux2", 1)}

    def omtcol(p, cc):
        t, blk = OMT[p]
        return acol(blk, cc) if t == "aux" else a2col(blk, cc)

    def xc(buf, cc):
        return buf[:, cc * TT:(cc + 1) * TT]

    P.op("sync", lambda e: e.dma_start(out=vecs[:], in_=vecs_in),
         writes=["vecs"], sem="cld", inc=16, dma=True)
    P.op("sync", lambda e: e.dma_start(out=h64f[:], in_=h64_in),
         writes=["h64f"], sem="cld", inc=16, dma=True)
    P.op("sync", lambda e: e.dma_start(out=h128f[:], in_=h128_in),
         writes=["h128f"], sem="cld", inc=16, dma=True)
    P.op("vector", lambda e: e.memset(onesb[:], 1.0), writes=["onesb"])
    P.op("vector", lambda e: e.memset(ones1[:], 1.0), writes=["ones1"])
    P.op("vector", lambda e: e.memset(epsr[:], EPS), writes=["epsr"])
    P.op("vector", lambda e: e.memset(acar[:], 0.0), writes=["acar"])
    P.op("vector", lambda e: e.memset(bcar[:], 0.0), writes=["bcar"])
    P.op("scalar", lambda e: e.copy(out=h128b[:], in_=h128f[:]),
         reads=["h128f"], writes=["h128b"])
    P.op("scalar", lambda e: e.activation(
            out=TP[0][:, 0:CC], in_=vecs[:, 0:CC], func=AF.Exp),
         reads=["vecs"], writes=["T0"])
    P.op("scalar", lambda e: e.activation(
            out=aux[:, 0:CC], in_=TP[0][:, 0:CC], func=AF.Exp, scale=-1.0),
         reads=["T0"], writes=["aux"])
    P.op("scalar", lambda e: e.activation(
            out=aux3[:, 0:CC], in_=vecs[:, CC:2 * CC], func=AF.Exp),
         reads=["vecs"], writes=["aux3"])
    for p in ["k", "v", "r", "g"]:
        t, blk = OMT[p]
        dst = aux if t == "aux" else aux2
        off = blk * CC
        P.op("vector", lambda e, dst=dst, off=off, p=p: e.tensor_scalar(
                out=dst[:, off:off + CC], in0=vecs[:, TM[p] * CC:(TM[p] + 1) * CC],
                scalar1=-1.0, scalar2=1.0, op0=OP.mult, op1=OP.add),
             reads=["vecs"], writes=[t])

    wcat_r = wcat.rearrange("(u p) f -> u p f", p=128)
    w_bf_r = w_bf.rearrange("(u p) f -> u p f", p=128)
    rp_state = {"seq": 0}

    w2t_r = w2t.rearrange("(hc p) co -> hc p co", p=128)
    w2s_r = w2s.rearrange("(hc p) co -> hc p co", p=128)
    n_ch = C // 1024

    tf_state = {"i": 0}
    fring = {"i": 0}

    def stage_f32(srcap):
        fsl = fring["i"] % 3
        feng = ("sync", "gpsimd", "sync")[fsl]
        fring["i"] += 1
        P.op(feng, lambda e, srcap=srcap, fsl=fsl: e.dma_start(
                out=rpf[fsl][:], in_=srcap),
             writes=[f"rpf{fsl}"], sem=f"rpl{fsl}", inc=16, dma=True)
        return fsl

    def emit_transform_pieces(n):
        w2tp = w2t.rearrange("(i p) f -> i p f", p=128)
        for _ in range(n):
            i = tf_state["i"]
            if i >= 64 * n_ch:
                return
            tf_state["i"] += 1
            hpp, ch = divmod(i, n_ch)
            hp0, hp1 = 2 * hpp, 2 * hpp + 1
            sl = i % 2
            P.op("gpsimd", lambda e, i=i, sl=sl, w2tp=w2tp: e.dma_start(
                    out=wtf[sl][:], in_=w2tp[i]),
                 writes=[f"wtf{sl}"], sem=f"wtl{sl}", inc=16, dma=True)
            for s2 in range(2):
                bk = 4 + s2
                o0 = sl * 1024 + s2 * 512
                P.op("tensor", lambda e, sl=sl, s2=s2, bk=bk: e.matmul(
                        pb[bk][:, :], h64f[:, :],
                        wtf[sl][:, s2 * 512:(s2 + 1) * 512],
                        start=True, stop=True),
                     reads=[f"wtf{sl}", "h64f"], writes=[f"pb{bk}"])
                P.op("vector", lambda e, o0=o0, bk=bk, sl=sl: e.tensor_scalar(
                        out=w2o2[:, o0:o0 + 512],
                        in0=pb[bk][:, :], scalar1=SCL, scalar2=None,
                        op0=OP.mult),
                     reads=[f"pb{bk}"], writes=[f"w2o2{sl}"])
            P.op("sync", lambda e, hp0=hp0, ch=ch, sl=sl: e.dma_start(
                    out=w2s_r[:, hp0, ch * 1024:(ch + 1) * 1024],
                    in_=w2o2[0:HC, sl * 1024:(sl + 1) * 1024]),
                 reads=[f"w2o2{sl}"], writes=[f"w2s_{ch}"], sem="w2st",
                 inc=16, dma=True)
            P.op("sync", lambda e, hp1=hp1, ch=ch, sl=sl: e.dma_start(
                    out=w2s_r[:, hp1, ch * 1024:(ch + 1) * 1024],
                    in_=w2o2[HC:2 * HC, sl * 1024:(sl + 1) * 1024]),
                 reads=[f"w2o2{sl}"], writes=[f"w2s_{ch}"], sem="w2st",
                 inc=16, dma=True)

    ring = {"seq": 0}
    first_use = set()
    rp_seq = {"i": 0}

    def fill_slot(units):
        sl = ring["seq"] % NWR
        ring["seq"] += 1
        name = f"wr{sl}"
        ch = f"cw{sl}"
        u0 = units[0]
        if (isinstance(u0, int) and u0 not in first_use
                and all(isinstance(u, int) for u in units)):
            assert len(units) == 2 and units[1] == u0 + 1
            first_use.update(units)
            for j, u in enumerate(units):
                fsl = stage_f32(wcat_r[u])
                P.op("vector", lambda e, sl=sl, fsl=fsl, j=j: e.tensor_copy(
                        out=wr[sl][:, j * 1024:(j + 1) * 1024],
                        in_=rpf[fsl][:]),
                     reads=[f"rpf{fsl}"], writes=[name])
            wdst = w_bf.rearrange("(u p) f -> p u f", p=128)[:, u0:u0 + 2, :]
            P.op("scalar", lambda e, sl=sl, wdst=wdst: e.dma_start(
                    out=wdst,
                    in_=wr[sl][:].rearrange("p (u f) -> p u f", f=1024)),
                 reads=[name], writes=[f"wbf_u{u0}", f"wbf_u{u0+1}"],
                 sem="rps", inc=16, dma=True)
            return sl, wr[sl]
        ieng = "sync" if sl % 2 == 0 else "gpsimd"
        for j, u in enumerate(units):
            if isinstance(u, int):
                P.op(ieng, lambda e, u=u, j=j, sl=sl: e.dma_start(
                        out=wr[sl][:, j * 1024:(j + 1) * 1024],
                        in_=w_bf_r[u]),
                     reads=[f"wbf_u{u}"], writes=[name], sem=ch, inc=16,
                     dma=True)
            else:
                r0, c0 = u
                P.op(ieng, lambda e, r0=r0, c0=c0, j=j, sl=sl: e.dma_start(
                        out=wr[sl][:, j * 1024:(j + 1) * 1024],
                        in_=w2s[r0:r0 + 128, c0:c0 + 1024]),
                     reads=[f"w2s_{c0 // 1024}"],
                     writes=[name], sem=ch, inc=16, dma=True)
        return sl, wr[sl]

    PIDX = {"k": 0, "v": 1, "r": 2, "g": 3, "o": 4}

    def proj_unit(p, h, kc):
        return (PIDX[p] * NHALF + h) * CC + kc

    def ffn_unit(q, ug, kc):
        return n_proj_units + (q * (HCQ // GRP) + ug) * CC + kc

    def load_x(tile, dst, dnames):
        sq, ti = divmod(tile, TPS)
        col0 = sq * T + ti * TT
        srcap = x_fm.rearrange("(cc p) n -> p cc n", p=128)[:, :, col0:col0 + TT]
        xdst = dst[:].rearrange("p (cc n) -> p cc n", n=TT)
        P.op("scalar", lambda e, srcap=srcap, xdst=xdst: e.dma_start(
                out=xdst, in_=srcap),
             reads=[], writes=dnames, sem="xld", inc=16, dma=True)

    load_x(0, xbuf, [f"xbuf_{c}" for c in range(CC)])
    for tile in range(NTILES):
        sq, ti = divmod(tile, TPS)
        t0 = ti * TT
        col0 = sq * T + t0
        bpar = tile % 2
        xb, xbn = (xbuf, "xbuf") if tile % 2 == 0 else (P1, "P1")
        pp, ppn = (P1, "P1") if tile % 2 == 0 else (xbuf, "xbuf")
        Xn = [f"{xbn}_{c}" for c in range(CC)]

        for cc in range(CC):
            cb = (cc % 2) * 2
            if cc % 2 == 0:
                P.op("scalar", lambda e, xb=xb, cc=cc, cb=cb: e.copy(
                        out=cbf[cb][:], in_=xc(xb, cc)),
                     reads=[f"{xbn}_{cc}"], writes=[f"cbf{cb}"])
                P.op("vector", lambda e, xb=xb, cc=cc, cb=cb: e.tensor_tensor(
                        out=cbf[cb + 1][:], in0=xc(xb, cc), in1=xc(xb, cc),
                        op=OP.mult),
                     reads=[f"{xbn}_{cc}"], writes=[f"cbf{cb+1}"])
            else:
                P.op("vector", lambda e, xb=xb, cc=cc, cb=cb: e.tensor_copy(
                        out=cbf[cb][:], in_=xc(xb, cc)),
                     reads=[f"{xbn}_{cc}"], writes=[f"cbf{cb}"])
                P.op("scalar", lambda e, xb=xb, cc=cc, cb=cb: e.activation(
                        out=cbf[cb + 1][:], in_=xc(xb, cc), func=AF.Square),
                     reads=[f"{xbn}_{cc}"], writes=[f"cbf{cb+1}"])
            P.op("tensor", lambda e, cc=cc, cb=cb: e.matmul(
                    pb[6][0:1, 0:TT], onesb[:], cbf[cb][:],
                    start=(cc == 0), stop=(cc == CC - 1)),
                 reads=[f"cbf{cb}", "onesb"], writes=["pb6"],
                 inc=1 if cc == CC - 1 else 0)
            P.op("tensor", lambda e, cc=cc, cb=cb: e.matmul(
                    pb[7][0:1, 0:TT], onesb[:], cbf[cb + 1][:],
                    start=(cc == 0), stop=(cc == CC - 1)),
                 reads=[f"cbf{cb+1}", "onesb"], writes=["pb7"],
                 inc=1)
        P.op("vector", lambda e, xb=xb, pp=pp: e.tensor_scalar(
                out=rows[:, 0:TT], in0=pb[6][0:1, 0:TT], scalar1=1.0 / C,
                scalar2=None, op0=OP.mult),
             reads=["pb6"], writes=["P2_0"])
        P.op("vector", lambda e, xb=xb, pp=pp: e.tensor_tensor(
                out=rows[:, 2 * TT:3 * TT], in0=rows[:, 0:TT],
                in1=rows[:, 0:TT], op=OP.mult),
             reads=["P2_0"], writes=["P2_2"])
        P.op("vector", lambda e, xb=xb, pp=pp: e.scalar_tensor_tensor(
                out=rows[:, 2 * TT:3 * TT], in0=pb[7][0:1, 0:TT],
                scalar=1.0 / C, in1=rows[:, 2 * TT:3 * TT],
                op0=OP.mult, op1=OP.subtract),
             reads=["pb7", "P2_2"], writes=["P2_2"])
        P.op("scalar", lambda e, xb=xb, pp=pp: e.activation(
                out=rows[:, 2 * TT:3 * TT], in_=rows[:, 2 * TT:3 * TT],
                func=AF.Ln, bias=epsr[0:1, 0:1]),
             reads=["P2_2", "epsr"], writes=["P2_2"])
        P.op("scalar", lambda e, xb=xb, pp=pp: e.activation(
                out=rows[:, TT:2 * TT], in_=rows[:, 2 * TT:3 * TT],
                func=AF.Exp, scale=-0.5),
             reads=["P2_2"], writes=["P2_1"])
        P.op("tensor", lambda e, xb=xb, pp=pp: e.matmul(pb[4][:, 0:TT], ones1[:],
                                          rows[:, 0:TT], start=True, stop=True),
             reads=["P2_0", "ones1"], writes=["pb4"])
        P.op("tensor", lambda e, xb=xb, pp=pp: e.matmul(pb[5][:, 0:TT], ones1[:],
                                          rows[:, TT:2 * TT], start=True,
                                          stop=True),
             reads=["P2_1", "ones1"], writes=["pb5"])
        P.op("vector", lambda e: e.tensor_copy(out=mubc[:], in_=pb[4][:, 0:TT]),
             reads=["pb4"], writes=["mubc"])
        P.op("vector", lambda e: e.tensor_copy(out=rsdbc[:], in_=pb[5][:, 0:TT]),
             reads=["pb5"], writes=["rsdbc"])

        for cc in range(CC):
            slot = cc % 2
            P.op("vector", lambda e, xb=xb, cc=cc, slot=slot: e.tensor_tensor(
                    out=TP[slot][:, 0:TT], in0=xc(xb, cc), in1=mubc[:],
                    op=OP.subtract),
                 reads=[f"{xbn}_{cc}", "mubc"], writes=[f"T{slot}"])
            P.op("vector", lambda e, cc=cc, slot=slot: e.tensor_tensor(
                    out=xc(attnG, cc), in0=TP[slot][:, 0:TT],
                    in1=rsdbc[:], op=OP.mult),
                 reads=[f"T{slot}", "rsdbc"], writes=[f"xnbf_{cc}"])
            if t0 == 0:
                P.op("vector", lambda e, cc=cc, bpar=bpar: e.tensor_copy(
                        out=xnb[bpar][:, cc:cc + 1], in_=xc(attnG, cc)[:, 1:2]),
                     reads=[f"xnbf_{cc}"], writes=[f"xnb{bpar}"])
            P.op("vector", lambda e, cc=cc, bpar=bpar: e.tensor_copy(
                    out=xnb[1 - bpar][:, cc:cc + 1],
                    in_=xc(attnG, cc)[:, TT - 1:TT]),
                 reads=[f"xnbf_{cc}"], writes=[f"xnb{1-bpar}"])

        def build_operand(p, B, Bn):
            for cc in range(CC):
                slot = cc % 2
                tn = f"T{slot}"
                P.op("vector", lambda e, cc=cc, slot=slot, p=p:
                     e.tensor_scalar(
                        out=TP[slot][:, 0:TT], in0=xc(attnG, cc),
                        scalar1=omtcol(p, cc), scalar2=None, op0=OP.mult),
                     reads=[f"xnbf_{cc}", OMT[p][0]], writes=[tn])
                P.op("vector", lambda e, cc=cc, slot=slot, p=p:
                     e.scalar_tensor_tensor(
                        out=xc(B, cc)[:, 1:TT], in0=xc(attnG, cc)[:, 0:TT - 1],
                        scalar=vcol(TM[p], cc), in1=TP[slot][:, 1:TT],
                        op0=OP.mult, op1=OP.add),
                     reads=[f"xnbf_{cc}", tn], writes=[f"{Bn}_{cc}"])
                P.op("vector", lambda e, cc=cc, slot=slot, p=p, bpar=bpar:
                     e.scalar_tensor_tensor(
                        out=xc(B, cc)[:, 0:1], in0=xnb[bpar][:, cc:cc + 1],
                        scalar=vcol(TM[p], cc), in1=TP[slot][:, 0:1],
                        op0=OP.mult, op1=OP.add),
                     reads=[f"xnb{bpar}", tn], writes=[f"{Bn}_{cc}"])
                if tile == 0:
                    emit_transform_pieces(2)

        def proj_matmuls(p, B, Bn, evac):
            for h in range(NHALF):
                ccs = range(h * 8, min((h + 1) * 8, CC))
                for kc in range(CC):
                    if kc % 2 == 0:
                        sl, wrt = fill_slot([proj_unit(p, h, kc),
                                             proj_unit(p, h, kc + 1)])
                    uoff = (kc % 2) * 1024
                    for ci, cc in enumerate(ccs):
                        P.op("tensor",
                             lambda e, xb=xb, pp=pp, sl=sl, uoff=uoff, ci=ci, kc=kc, cc=cc:
                             e.matmul(
                                pb[ci][:, 0:TT],
                                wr[sl][:, uoff + ci * 128:uoff + (ci + 1) * 128],
                                xc(B, kc),
                                start=(kc == 0), stop=(kc == CC - 1)),
                             reads=[f"wr{sl}", f"{Bn}_{kc}"],
                             writes=[f"pb{ci}"],
                             inc=1 if (kc == CC - 1 or
                                       (kc % 2 == 1 and ci == len(ccs) - 1))
                             else 0)
                for ci, cc in enumerate(ccs):
                    evac(cc, ci)

        def ffn_up_group(q, ug):
            for kc in range(CC):
                if kc % 2 == 0:
                    sl, wrt = fill_slot([ffn_unit(q, ug, kc),
                                         ffn_unit(q, ug, kc + 1)])
                uoff = (kc % 2) * 1024
                for hj in range(GRP):
                    P.op("tensor",
                         lambda e, sl=sl, uoff=uoff, hj=hj, kc=kc:
                         e.matmul(
                            pb[2 * hj][:, 0:TT],
                            wr[sl][:, uoff + hj * 128:uoff + (hj + 1) * 128],
                            xc(attnG, kc), start=(kc == 0),
                            stop=(kc == CC - 1)),
                         reads=[f"wr{sl}", f"xnbf_{kc}"],
                         writes=[f"pb{2*hj}"],
                         inc=1 if kc == CC - 1 else 0)
                    P.op("tensor",
                         lambda e, sl=sl, uoff=uoff, hj=hj, kc=kc:
                         e.matmul(
                            pb[2 * hj + 1][:, 0:TT],
                            wr[sl][:, uoff + 512 + hj * 128:
                                   uoff + 512 + (hj + 1) * 128],
                            xc(attnG, kc), start=(kc == 0),
                            stop=(kc == CC - 1)),
                         reads=[f"wr{sl}", f"xnbf_{kc}"],
                         writes=[f"pb{2*hj+1}"],
                         inc=1 if (kc == CC - 1 or
                                   (kc % 2 == 1 and hj == GRP - 1))
                         else 0)
            for hj in range(GRP):
                j = ug * GRP + hj
                slot = hj % 2
                P.op("scalar", lambda e, hj=hj, slot=slot: e.activation(
                        out=TP[slot][:, 0:TT], in_=pb[2 * hj][:, 0:TT],
                        func=AF.Silu),
                     reads=[f"pb{2*hj}"], writes=[f"T{slot}"])
                P.op("vector", lambda e, hj=hj, j=j, slot=slot:
                     e.tensor_tensor(
                        out=xc(B2, j), in0=TP[slot][:, 0:TT],
                        in1=pb[2 * hj + 1][:, 0:TT], op=OP.mult),
                     reads=[f"T{slot}", f"pb{2*hj+1}"],
                     writes=[f"B2_{j}"])

        def ffn_h128(q):
            for j in range(HCQ):
                bk = j % 4
                P.op("tensor", lambda e, j=j, bk=bk: e.matmul(
                        pb[bk][:, 0:TT], h128b[:], xc(B2, j),
                        start=True, stop=True),
                     reads=["h128b", f"B2_{j}"], writes=[f"pb{bk}"])
                P.op("scalar", lambda e, j=j, bk=bk: e.activation(
                        out=xc(B2, j), in_=pb[bk][:, 0:TT], func=AF.Copy,
                        scale=1.0),
                     reads=[f"pb{bk}"], writes=[f"B2_{j}"])

        def ffn_down(q):
            for h in range(NHALF):
                ccs = range(h * 8, min((h + 1) * 8, CC))
                for j in range(HCQ):
                    kc2 = q * HCQ + j
                    if j % 2 == 0:
                        sl, wrt = fill_slot([(kc2 * 128, h * 1024),
                                             ((kc2 + 1) * 128, h * 1024)])
                    uoff = (j % 2) * 1024
                    for ci, cc in enumerate(ccs):
                        P.op("tensor",
                             lambda e, sl=sl, uoff=uoff, ci=ci, j=j:
                             e.matmul(
                                pb[ci][:, 0:TT],
                                wr[sl][:, uoff + ci * 128:uoff + (ci + 1) * 128],
                                xc(B2, j), start=(j == 0),
                                stop=(j == HCQ - 1)),
                             reads=[f"wr{sl}", f"B2_{j}"], writes=[f"pb{ci}"],
                             inc=1 if (j == HCQ - 1 or
                                       (j % 2 == 1 and ci == len(ccs) - 1))
                             else 0)
                for ci, cc in enumerate(ccs):
                    P.op("vector", lambda e, xb=xb, cc=cc, ci=ci: e.tensor_tensor(
                            out=xc(xb, cc), in0=xc(xb, cc),
                            in1=pb[ci][:, 0:TT], op=OP.add),
                         reads=[f"{xbn}_{cc}", f"pb{ci}"], writes=[f"{xbn}_{cc}"])

        build_operand("k", B1, "B1")

        def evac_k(cc, bank):
            P.op("scalar", lambda e, pp=pp, cc=cc, bank=bank: e.copy(
                    out=xc(pp, cc), in_=pb[bank][:, 0:TT]),
                 reads=[f"pb{bank}"], writes=[f"{ppn}_{cc}"])
        proj_matmuls("k", B1, "B1", evac_k)

        build_operand("v", B2, "B2")

        def evac_v(cc, bank):
            P.op("scalar", lambda e, cc=cc, bank=bank: e.copy(
                    out=xc(P2, cc), in_=pb[bank][:, 0:TT]),
                 reads=[f"pb{bank}"], writes=[f"P2_{cc}"])
        proj_matmuls("v", B2, "B2", evac_v)

        build_operand("r", B1, "B1")

        if ti == 0:
            P.op("vector", lambda e: e.memset(acar[:], 0.0), writes=["acar"])
            P.op("vector", lambda e: e.memset(bcar[:], 0.0), writes=["bcar"])
        for cc in range(CC):
            P.op("scalar", lambda e, pp=pp, cc=cc: e.activation(
                    out=TP[2][:, 0:TT], in_=xc(pp, cc), func=AF.Exp),
                 reads=[f"{ppn}_{cc}"], writes=["T2"])
            P.op("vector", lambda e, cc=cc: e.tensor_tensor(
                    out=xc(P2, cc), in0=TP[2][:, 0:TT], in1=xc(P2, cc),
                    op=OP.mult),
                 reads=["T2", f"P2_{cc}"], writes=[f"P2_{cc}"])
            P.op("vector", lambda e, cc=cc: e.tensor_copy(
                    out=abuf[:, 0:1], in_=acar[:, cc:cc + 1]),
                 reads=["acar"], writes=["abuf"])
            P.op("vector", lambda e, cc=cc: e.tensor_tensor_scan(
                    out=abuf[:, 1:TT + 1],
                    data0=acol(0, cc).broadcast_to((128, TT)),
                    data1=xc(P2, cc), initial=acar[:, cc:cc + 1],
                    op0=OP.mult, op1=OP.add),
                 reads=["aux", f"P2_{cc}", "acar"], writes=["abuf"])
            P.op("vector", lambda e, cc=cc: e.tensor_copy(
                    out=acar[:, cc:cc + 1], in_=abuf[:, TT:TT + 1]),
                 reads=["abuf"], writes=["acar"])
            P.op("vector", lambda e, cc=cc: e.tensor_copy(
                    out=bbuf[:, 0:1], in_=bcar[:, cc:cc + 1]),
                 reads=["bcar"], writes=["bbuf"])
            P.op("vector", lambda e, cc=cc: e.tensor_tensor_scan(
                    out=bbuf[:, 1:TT + 1],
                    data0=acol(0, cc).broadcast_to((128, TT)),
                    data1=TP[2][:, 0:TT], initial=bcar[:, cc:cc + 1],
                    op0=OP.mult, op1=OP.add),
                 reads=["aux", "T2", "bcar"], writes=["bbuf"])
            P.op("vector", lambda e, cc=cc: e.tensor_copy(
                    out=bcar[:, cc:cc + 1], in_=bbuf[:, TT:TT + 1]),
                 reads=["bbuf"], writes=["bcar"])
            P.op("vector", lambda e, cc=cc: e.scalar_tensor_tensor(
                    out=abuf[:, 0:TT], in0=xc(P2, cc),
                    scalar=aux3[:, cc:cc + 1], in1=abuf[:, 0:TT],
                    op0=OP.mult, op1=OP.add),
                 reads=[f"P2_{cc}", "aux3", "abuf"], writes=["abuf"])
            P.op("vector", lambda e, cc=cc: e.scalar_tensor_tensor(
                    out=bbuf[:, 0:TT], in0=TP[2][:, 0:TT],
                    scalar=aux3[:, cc:cc + 1], in1=bbuf[:, 0:TT],
                    op0=OP.mult, op1=OP.add),
                 reads=["T2", "aux3", "bbuf"], writes=["bbuf"])
            P.op("scalar", lambda e, cc=cc: e.activation(
                    out=TP[2][:, 0:TT], in_=bbuf[:, 0:TT], func=AF.Ln),
                 reads=["bbuf"], writes=["T2"])
            P.op("scalar", lambda e, cc=cc: e.activation(
                    out=TP[2][:, 0:TT], in_=TP[2][:, 0:TT], func=AF.Exp,
                    scale=-1.0),
                 reads=["T2"], writes=["T2"])
            P.op("vector", lambda e, pp=pp, cc=cc: e.tensor_tensor(
                    out=xc(pp, cc), in0=abuf[:, 0:TT], in1=TP[2][:, 0:TT],
                    op=OP.mult),
                 reads=["abuf", "T2"], writes=[f"{ppn}_{cc}"])
            if cc % GRP == GRP - 1 and tile > 0:
                ffn_up_group(0, cc // GRP)
            if tile == 0:
                emit_transform_pieces(2)

        if tile > 0:
            ffn_h128(0)
            ffn_down(0)

        def evac_r(cc, bank):
            slot = cc % 2
            P.op("scalar", lambda e, cc=cc, bank=bank, slot=slot:
                 e.activation(out=TP[slot][:, 0:TT], in_=pb[bank][:, 0:TT],
                              func=AF.Sigmoid),
                 reads=[f"pb{bank}"], writes=[f"T{slot}"])
            P.op("vector", lambda e, pp=pp, cc=cc, slot=slot: e.tensor_tensor(
                    out=xc(pp, cc), in0=xc(pp, cc), in1=TP[slot][:, 0:TT],
                    op=OP.mult),
                 reads=[f"{ppn}_{cc}", f"T{slot}"], writes=[f"{ppn}_{cc}"])
        proj_matmuls("r", B1, "B1", evac_r)

        build_operand("g", B2, "B2")

        def evac_g(cc, bank):
            slot = cc % 2
            P.op("scalar", lambda e, cc=cc, bank=bank, slot=slot:
                 e.activation(out=TP[slot][:, 0:TT], in_=pb[bank][:, 0:TT],
                              func=AF.Sigmoid),
                 reads=[f"pb{bank}"], writes=[f"T{slot}"])
            P.op("vector", lambda e, pp=pp, cc=cc, slot=slot: e.tensor_tensor(
                    out=xc(B1, cc), in0=xc(pp, cc), in1=TP[slot][:, 0:TT],
                    op=OP.mult),
                 reads=[f"{ppn}_{cc}", f"T{slot}"], writes=[f"B1_{cc}"])
        proj_matmuls("g", B2, "B2", evac_g)

        def evac_o(cc, bank):
            P.op("scalar", lambda e, pp=pp, cc=cc, bank=bank: e.copy(
                    out=xc(pp, cc), in_=pb[bank][:, 0:TT]),
                 reads=[f"pb{bank}"], writes=[f"{ppn}_{cc}"])
        for h in range(NHALF):
            ccs = range(h * 8, min((h + 1) * 8, CC))
            for kc in range(CC):
                if kc % 2 == 0:
                    sl, wrt = fill_slot([proj_unit("o", h, kc),
                                         proj_unit("o", h, kc + 1)])
                uoff = (kc % 2) * 1024
                for ci, cc in enumerate(ccs):
                    P.op("tensor", lambda e, sl=sl, uoff=uoff, ci=ci, kc=kc:
                         e.matmul(
                            pb[ci][:, 0:TT],
                            wr[sl][:, uoff + ci * 128:uoff + (ci + 1) * 128],
                            xc(B1, kc), start=(kc == 0),
                            stop=(kc == CC - 1)),
                         reads=[f"wr{sl}", f"B1_{kc}"], writes=[f"pb{ci}"],
                         inc=1 if (kc == CC - 1 or
                                   (kc % 2 == 1 and ci == len(ccs) - 1))
                         else 0)
            for ci, cc in enumerate(ccs):
                evac_o(cc, ci)

        for cc in range(CC):
            cb = (cc % 2) * 2
            if cc % 2 == 0:
                P.op("scalar", lambda e, pp=pp, cc=cc, cb=cb: e.copy(
                        out=cbf[cb][:], in_=xc(pp, cc)),
                     reads=[f"{ppn}_{cc}"], writes=[f"cbf{cb}"])
                P.op("vector", lambda e, pp=pp, cc=cc, cb=cb: e.tensor_tensor(
                        out=cbf[cb + 1][:], in0=xc(pp, cc), in1=xc(pp, cc),
                        op=OP.mult),
                     reads=[f"{ppn}_{cc}"], writes=[f"cbf{cb+1}"])
            else:
                P.op("vector", lambda e, pp=pp, cc=cc, cb=cb: e.tensor_copy(
                        out=cbf[cb][:], in_=xc(pp, cc)),
                     reads=[f"{ppn}_{cc}"], writes=[f"cbf{cb}"])
                P.op("scalar", lambda e, pp=pp, cc=cc, cb=cb: e.activation(
                        out=cbf[cb + 1][:], in_=xc(pp, cc), func=AF.Square),
                     reads=[f"{ppn}_{cc}"], writes=[f"cbf{cb+1}"])
            P.op("tensor", lambda e, cc=cc, cb=cb: e.matmul(
                    pb[6][0:1, 0:TT], onesb[:], cbf[cb][:],
                    start=(cc == 0), stop=(cc == CC - 1)),
                 reads=[f"cbf{cb}", "onesb"], writes=["pb6"],
                 inc=1 if cc == CC - 1 else 0)
            P.op("tensor", lambda e, cc=cc, cb=cb: e.matmul(
                    pb[7][0:1, 0:TT], onesb[:], cbf[cb + 1][:],
                    start=(cc == 0), stop=(cc == CC - 1)),
                 reads=[f"cbf{cb+1}", "onesb"], writes=["pb7"],
                 inc=1)
        P.op("vector", lambda e: e.tensor_scalar(
                out=rows[:, 2 * TT:3 * TT], in0=pb[6][0:1, 0:TT],
                scalar1=1.0 / C, scalar2=None, op0=OP.mult),
             reads=["pb6"], writes=["P2_2"])
        P.op("vector", lambda e: e.tensor_tensor(
                out=rows[:, 0:TT], in0=rows[:, 2 * TT:3 * TT],
                in1=rows[:, 2 * TT:3 * TT], op=OP.mult),
             reads=["P2_2"], writes=["P2_0"])
        P.op("vector", lambda e: e.scalar_tensor_tensor(
                out=rows[:, 0:TT], in0=pb[7][0:1, 0:TT],
                scalar=1.0 / C, in1=rows[:, 0:TT],
                op0=OP.mult, op1=OP.subtract),
             reads=["pb7", "P2_0"], writes=["P2_0"])
        P.op("scalar", lambda e: e.activation(
                out=rows[:, 0:TT], in_=rows[:, 0:TT],
                func=AF.Ln, bias=epsr[0:1, 0:1]),
             reads=["P2_0", "epsr"], writes=["P2_0"])
        P.op("scalar", lambda e: e.activation(
                out=rows[:, 0:TT], in_=rows[:, 0:TT],
                func=AF.Exp, scale=-0.5),
             reads=["P2_0"], writes=["P2_0"])
        P.op("tensor", lambda e: e.matmul(pb[4][:, 0:TT], ones1[:],
                                          rows[:, 2 * TT:3 * TT], start=True,
                                          stop=True),
             reads=["P2_2", "ones1"], writes=["pb4"])
        P.op("tensor", lambda e: e.matmul(pb[5][:, 0:TT], ones1[:],
                                          rows[:, 0:TT], start=True,
                                          stop=True),
             reads=["P2_0", "ones1"], writes=["pb5"])
        P.op("vector", lambda e: e.tensor_copy(out=mubc[:], in_=pb[4][:, 0:TT]),
             reads=["pb4"], writes=["mubc"])
        P.op("vector", lambda e: e.tensor_copy(out=rsdbc[:], in_=pb[5][:, 0:TT]),
             reads=["pb5"], writes=["rsdbc"])
        for cc in range(CC):
            slot = cc % 2
            P.op("vector", lambda e, xb=xb, cc=cc: e.tensor_scalar(
                    out=xc(xb, cc), in0=xc(xb, cc), scalar1=vcol(7, cc),
                    scalar2=None, op0=OP.add),
                 reads=[f"{xbn}_{cc}", "vecs"], writes=[f"{xbn}_{cc}"])
            P.op("vector", lambda e, pp=pp, cc=cc, slot=slot: e.tensor_tensor(
                    out=TP[slot][:, 0:TT], in0=xc(pp, cc), in1=mubc[:],
                    op=OP.subtract),
                 reads=[f"{ppn}_{cc}", "mubc"], writes=[f"T{slot}"])
            P.op("vector", lambda e, cc=cc, slot=slot: e.tensor_tensor(
                    out=TP[slot][:, 0:TT], in0=TP[slot][:, 0:TT],
                    in1=rsdbc[:], op=OP.mult),
                 reads=[f"T{slot}", "rsdbc"], writes=[f"T{slot}"])
            P.op("vector", lambda e, xb=xb, cc=cc, slot=slot: e.scalar_tensor_tensor(
                    out=xc(xb, cc), in0=TP[slot][:, 0:TT],
                    scalar=vcol(6, cc), in1=xc(xb, cc),
                    op0=OP.mult, op1=OP.add),
                 reads=[f"T{slot}", "vecs", f"{xbn}_{cc}"], writes=[f"{xbn}_{cc}"])

        if tile + 1 < NTILES:
            load_x(tile + 1, pp, [f"{ppn}_{c}" for c in range(CC)])

        for q in range(0 if tile == 0 else 1, HQ):
            for ug in range(HCQ // GRP):
                ffn_up_group(q, ug)
            ffn_h128(q)
            ffn_down(q)

        dst = y_fm.rearrange("(cc p) n -> p cc n", p=128)[:, :, col0:col0 + TT]
        xsrc = xb[:].rearrange("p (cc n) -> p cc n", n=TT)
        P.op("scalar", lambda e, xb=xb, pp=pp, dst=dst, xsrc=xsrc: e.dma_start(out=dst, in_=xsrc),
             reads=Xn, writes=[], sem="xst", inc=16, dma=True)

    P.wait("sync", "xst", 16 * NTILES)
    P.build()
    meta = dict(C=C, H=H, T=T, SEQS=SEQS, TT=TT, n_units=n_units,
                n_proj_units=n_proj_units, NHALF=NHALF, CC=CC, HC=HC,
                HQ=HQ, HCQ=HCQ, GRP=GRP)
    return nc, meta



def pack_weights(Wr, Wk, Wv, Wg, Wo, W1, W3, meta):
    CC, NHALF, HQ, HCQ, GRP = (meta["CC"], meta["NHALF"], meta["HQ"],
                               meta["HCQ"], meta["GRP"])
    units = []
    for W in [Wk, Wv, Wr, Wg, Wo]:
        WT = np.ascontiguousarray(W.T)
        for h in range(NHALF):
            for kc in range(CC):
                u = WT[kc * 128:(kc + 1) * 128, h * 1024:(h + 1) * 1024]
                units.append(u)
    W1T = np.ascontiguousarray(W1.T)
    W3T = np.ascontiguousarray(W3.T)
    for q in range(HQ):
        for ug in range(HCQ // GRP):
            base = (q * HCQ + ug * GRP) * 128
            for kc in range(CC):
                u = np.concatenate(
                    [W1T[kc * 128:(kc + 1) * 128, base:base + 512],
                     W3T[kc * 128:(kc + 1) * 128, base:base + 512]], axis=1)
                units.append(u)
    assert len(units) == meta["n_units"]
    return np.ascontiguousarray(np.concatenate(units, axis=0))


def pack_w2t(W2, meta):
    HC = meta["HC"]
    C = W2.shape[0]
    n_ch = C // 1024
    W2T = np.ascontiguousarray(W2.T)
    v = W2T.reshape(HC, 64, 2, n_ch, 1024)
    v = v.transpose(1, 3, 2, 0, 4)
    return np.ascontiguousarray(v.reshape(64 * n_ch * 128, 1024))


def hadamard_blockdiag128(hc):
    h = hadamard(hc)
    d = np.zeros((128, 128), np.float32)
    d[:hc, :hc] = h
    if 2 * hc <= 128:
        d[hc:2 * hc, hc:2 * hc] = h
    return d


def hadamard(n):
    h = np.array([[1.0]], dtype=np.float32)
    while h.shape[0] < n:
        h = np.block([[h, h], [h, -h]])
    return h.astype(np.float32)


def pack_vecs(td, tf, tmk, tmv, tmr, tmg, lng, lnb, CC):
    cols = []
    for v in [td, tf, tmk, tmv, tmr, tmg, lng, lnb]:
        cols.append(np.asarray(v, np.float32).reshape(CC, 128).T)
    return np.ascontiguousarray(np.concatenate(cols, axis=1))



_CACHE = {}


def kernel(L0_attn_x, L0_attn_time_decay, L0_attn_time_first, L0_attn_tm_k,
           L0_attn_tm_v, L0_attn_tm_r, L0_attn_tm_g, L0_attn_Wr, L0_attn_Wk,
           L0_attn_Wv, L0_attn_Wg, L0_attn_Wo, L0_attn_ln_g, L0_attn_ln_b,
           L0_ffn_W1, L0_ffn_W2, L0_ffn_W3):
    x = np.asarray(L0_attn_x, np.float32)
    B, T, C = x.shape
    H = L0_ffn_W1.shape[0]
    n_cores = 8
    SEQS = B // n_cores

    if "prog" not in _CACHE:
        _CACHE["prog"] = build_program(C=C, H=H, T=T, SEQS=SEQS, TT=512)
    nc, meta = _CACHE["prog"]

    wcat = pack_weights(np.asarray(L0_attn_Wr, np.float32),
                        np.asarray(L0_attn_Wk, np.float32),
                        np.asarray(L0_attn_Wv, np.float32),
                        np.asarray(L0_attn_Wg, np.float32),
                        np.asarray(L0_attn_Wo, np.float32),
                        np.asarray(L0_ffn_W1, np.float32),
                        np.asarray(L0_ffn_W3, np.float32), meta)
    w2t = pack_w2t(np.asarray(L0_ffn_W2, np.float32), meta)
    vecs = pack_vecs(L0_attn_time_decay, L0_attn_time_first, L0_attn_tm_k,
                     L0_attn_tm_v, L0_attn_tm_r, L0_attn_tm_g, L0_attn_ln_g,
                     L0_attn_ln_b, meta["CC"])
    h64 = hadamard(meta["HC"])
    h64d = np.zeros((128, 128), np.float32)
    h64d[:64, :64] = h64
    h64d[64:, 64:] = h64
    h128 = hadamard(128)

    in_maps = []
    for c in range(n_cores):
        xs = x[c * SEQS:(c + 1) * SEQS]
        x_fm = np.ascontiguousarray(
            xs.transpose(2, 0, 1).reshape(C, SEQS * T))
        in_maps.append({"x_fm": x_fm, "wcat": wcat, "w2t": w2t,
                        "vecs": vecs, "h64": h64d, "h128": h128})

    res = run_bass_kernel_spmd(nc, in_maps, core_ids=list(range(n_cores)))
    out = np.empty((B, T, C), np.float32)
    for c in range(n_cores):
        y = np.asarray(res.results[c]["y_fm"], np.float32)
        out[c * SEQS:(c + 1) * SEQS] = (
            y.reshape(C, SEQS, T).transpose(1, 2, 0))
    return out


def _run(nc, ins, n_cores=1, sim=False, trace=False):
    if sim:
        from concourse.bass_interp import CoreSim
        from concourse.libnrt import get_pod_config
        nc.finalize()
        sim_ = CoreSim(nc, require_finite=False, require_nnan=False,
                       **get_pod_config())
        for k, v in ins[0].items():
            sim_.tensor(k)[:] = v
        sim_.simulate(check_with_hw=False)
        return [dict(_sim=sim_)]
    res = run_bass_kernel_spmd(nc, ins, core_ids=list(range(n_cores)),
                               trace=trace)
    if trace:
        return res
    return res.results
